# version 07
# baseline (speedup 1.0000x reference)
import contextlib
import numpy as np
import concourse.bass as bass
import concourse.mybir as mybir
from concourse.bass_utils import run_bass_kernel_spmd

F32 = mybir.dt.float32
BF16 = mybir.dt.bfloat16
AF = mybir.ActivationFunctionType
ALU = mybir.AluOpType
AX = mybir.AxisListType

P = 128
DM = 1024
DG = 8
S = 16
R = 64
NB = 2
NCORE = 8
H = 4096
NE = 8
NQ = 4
HQ = H // NQ
HC = HQ // P
EPS = 1e-5


class Ctr:
    def __init__(self, ctx, nc, name):
        self.h = ctx.enter_context(nc.semaphore(name))
        self.n = 0
        self.seen = {}

    def inc(self, instr, k=1):
        if id(instr) in self.seen:
            return self.n
        instr.then_inc(self.h, k)
        self.n += k
        return self.n

    def wait(self, eng, at=None):
        eng.wait_ge(self.h, self.n if at is None else at)


class ChainEng:
    _PASS = frozenset(("wait_ge", "drain"))
    _NOAUTO = frozenset(("dma_start", "collective_compute"))

    def __init__(self, eng, ctr):
        self._e = eng
        self._c = ctr

    def __getattr__(self, name):
        f = getattr(self._e, name)
        if name in ChainEng._PASS or name in ChainEng._NOAUTO \
                or not callable(f):
            return f
        c = self._c

        def g(*a, **k):
            n0 = c.n
            ins = f(*a, **k)
            if n0:
                ins._wait_ge(c.h, n0)
            c.inc(ins)
            c.seen[id(ins)] = ins
            return ins
        return g


def chained_block(deco, ctr):
    def d(fn):
        def inner(eng):
            return fn(ChainEng(eng, ctr))
        return deco(inner)
    return d


def build_nc(CL=1024, TC=256, keep=False):
    TOK = NB * CL
    NTC = TOK // TC
    TCB = CL // TC
    NTT = TOK // P
    N5 = min(512, TOK)
    NT5 = TOK // N5
    T128 = N5 // P
    PACKC = 2 * P + 2 * DG

    nc = bass.Bass("TRN2", target_bir_lowering=False)

    x_h = nc.dram_tensor("x01", [TOK, DM], F32, kind="ExternalInput")
    wxt_h = nc.dram_tensor("WxT", [DM, 96], F32, kind="ExternalInput")
    dtwt_h = nc.dram_tensor("dtWT", [R, DM], F32, kind="ExternalInput")
    a_h = nc.dram_tensor("A_pd", [P, DG, S], F32, kind="ExternalInput")
    d_h = nc.dram_tensor("D_pd", [P, DG], F32, kind="ExternalInput")
    dtb_h = nc.dram_tensor("dtb_pd", [P, DG], F32, kind="ExternalInput")
    mask_h = nc.dram_tensor("maskv", [P, NCORE], F32, kind="ExternalInput")
    id_h = nc.dram_tensor("ident", [P, P], F32, kind="ExternalInput")
    wr_h = nc.dram_tensor("Wr", [DM, NE], F32, kind="ExternalInput")
    br_h = nc.dram_tensor("br_r", [1, NE], F32, kind="ExternalInput")
    b2_h = nc.dram_tensor("b2", [NE, DM], F32, kind="ExternalInput")
    b1_h = nc.dram_tensor("b1r", [NE, P, H // P], F32, kind="ExternalInput")
    w1_h = nc.dram_tensor("W1", [NE, DM, H], F32, kind="ExternalInput")
    w2_h = nc.dram_tensor("W2", [NE, H, DM], F32, kind="ExternalInput")
    out_h = nc.dram_tensor("out", [TOK, DM], F32, kind="ExternalOutput")
    cci_h = nc.dram_tensor("cc_in", [P * PACKC], F32, kind="Internal")
    rstd_d = nc.dram_tensor("rstd_d", [NB * CL], F32, kind="Internal")
    cco_h = nc.dram_tensor("cc_out", [NCORE * P * PACKC], F32, kind="Internal")

    ctx = contextlib.ExitStack()
    with ctx:
        E = ctx.enter_context

        sX = Ctr(ctx, nc, "sX")
        sXD = [Ctr(ctx, nc, "sXD0"), Ctr(ctx, nc, "sXD1")]
        sR = Ctr(ctx, nc, "sR")
        sB1 = Ctr(ctx, nc, "sB1")
        sRS = Ctr(ctx, nc, "sRS")
        sW1 = Ctr(ctx, nc, "sW1")
        sW2 = Ctr(ctx, nc, "sW2")
        sT = Ctr(ctx, nc, "sT")
        sA = Ctr(ctx, nc, "sA")
        sV = Ctr(ctx, nc, "sV")
        sG = Ctr(ctx, nc, "sG")
        sC = Ctr(ctx, nc, "sC")
        sCC = Ctr(ctx, nc, "sCC")
        sO = [Ctr(ctx, nc, "sO0"), Ctr(ctx, nc, "sO1")]

        id_s = E(nc.sbuf_tensor("id_s", [P, P], F32))
        ones_all = E(nc.sbuf_tensor("ones_all", [P, P], F32))
        a_s = E(nc.sbuf_tensor("a_s", [P, DG, S], F32))
        dp1_s = E(nc.sbuf_tensor("dp1_s", [P, DG], F32))
        dtb_s = E(nc.sbuf_tensor("dtb_s", [P, DG], F32))
        mask_s = E(nc.sbuf_tensor("mask_s", [P, NCORE], F32))
        wxtb_s = E(nc.sbuf_tensor("wxtb_s", [P, DG, 96], BF16))
        dtwtb_s = E(nc.sbuf_tensor("dtwtb_s", [R, DM], BF16))
        wrb_s = E(nc.sbuf_tensor("wrb_s", [P, DG, NE], BF16))
        br_s = E(nc.sbuf_tensor("br_s", [1, NE], F32))
        b2_s = E(nc.sbuf_tensor("b2_s", [NE, DM], F32))
        b1r_s = E(nc.sbuf_tensor("b1r_s", [P, 2, H // P], F32))
        zeros_s = E(nc.sbuf_tensor("zeros_s", [P, TC], F32))
        state_s = E(nc.sbuf_tensor("state_s", [P, DG, S], F32))
        sstate_s = E(nc.sbuf_tensor("sstate_s", [P, DG], F32))
        ssum_s = E(nc.sbuf_tensor("ssum_s", [P, NB, DG], F32))
        hT_s = E(nc.sbuf_tensor("hT_s", [P, NB, P], F32))
        hin_s = E(nc.sbuf_tensor("hin_s", [P, NB, DG, S], F32))
        red_s = E(nc.sbuf_tensor("red_s", [P, 8], F32))
        rscr_s = E(nc.sbuf_tensor("rscr_s", [P, 24], F32))
        rfm_s = E(nc.sbuf_tensor("rfm_s", [NE, TOK], F32))
        rtok_s = E(nc.sbuf_tensor("rtok_s", [P, NTT, NE], F32))
        rstdt_s = E(nc.sbuf_tensor("rstdt_s", [P, NTT], F32))
        edg_s = E(nc.sbuf_tensor("edg_s", [P, DG, S], F32))
        tdg_s = E(nc.sbuf_tensor("tdg_s", [P, DG], F32))
        thp_s = E(nc.sbuf_tensor("thp_s", [P, P], F32))
        eps_s = E(nc.sbuf_tensor("eps_s", [P, 1], F32))

        pb = [E(nc.psum_tensor(f"pb{i}", [P, 512], F32)) for i in range(8)]
        bank_free = [None] * 8
        bank_prod = [None] * 8

        def bwait_free(eng, bk):
            if bank_free[bk] is not None:
                c, v = bank_free[bk]
                c.wait(eng, v)

        def bwait_prod(eng, bk):
            c, v = bank_prod[bk]
            c.wait(eng, v)

        block = E(nc.Block())
        bvec = chained_block(block.vector, sV)
        bscal = chained_block(block.scalar, sA)
        bgps = chained_block(block.gpsimd, sG)

        xT_scope = contextlib.ExitStack()
        xT = xT_scope.enter_context(nc.sbuf_tensor("xT", [P, DG, TOK], F32))

        mscope = contextlib.ExitStack()
        Em = mscope.enter_context
        wxts_s = Em(nc.sbuf_tensor("wxts_s", [P, DG, 96], F32))
        dtwts_s = Em(nc.sbuf_tensor("dtwts_s", [R, DM], F32))
        wrs_s = Em(nc.sbuf_tensor("wrs_s", [P, DG, NE], F32))
        proj_s = Em(nc.sbuf_tensor("proj_s", [96, TOK], F32))
        projb_s = Em(nc.sbuf_tensor("projb_s", [R, TOK], BF16))
        gath_s = Em(nc.sbuf_tensor("gath_s", [P, NCORE, PACKC], F32))
        pack_s = Em(nc.sbuf_tensor("pack_s", [P, PACKC], F32))

        @block.sync
        def _(sync):
            for dst, src in ((id_s, id_h), (a_s, a_h), (dp1_s, d_h),
                             (dtb_s, dtb_h), (mask_s, mask_h), (br_s, br_h),
                             (b2_s, b2_h)):
                sX.inc(sync.dma_start(out=dst[:], in_=src.ap()), 16)
            sX.inc(sync.dma_start(
                out=wxts_s[:],
                in_=wxt_h.ap().rearrange("(g p) m -> p g m", p=P)), 16)
            sX.inc(sync.dma_start(out=dtwts_s[:], in_=dtwt_h.ap()), 16)
            sX.inc(sync.dma_start(
                out=wrs_s[:],
                in_=wr_h.ap().rearrange("(g p) e -> p g e", p=P)), 16)
        v_const = sX.n

        @bscal
        def _(scalar):
            sX.wait(scalar, v_const)
            sA.inc(scalar.activation(out=a_s[:], in_=a_s[:], func=AF.Exp))
        v_aexp = sA.n

        @bvec
        def _(vector):
            sX.wait(vector, v_const)
            sA.wait(vector, v_aexp)
            vector.tensor_scalar_mul(a_s[:], a_s[:], -1.0)
            vector.tensor_scalar_add(dp1_s[:], dp1_s[:], 1.0)
            vector.memset(ones_all[:], 1.0)
            vector.memset(zeros_s[:], 0.0)
            vector.memset(state_s[:], 0.0)
            vector.memset(sstate_s[:], 0.0)
            vector.memset(ssum_s[:], 0.0)
            vector.memset(hin_s[:], 0.0)
            vector.memset(eps_s[:], EPS)
            vector.tensor_copy(out=wxtb_s[:], in_=wxts_s[:])
            vector.tensor_copy(out=dtwtb_s[:], in_=dtwts_s[:])
            sV.inc(vector.tensor_copy(out=wrb_s[:], in_=wrs_s[:]))
        v_setup = sV.n

        xscope = contextlib.ExitStack()
        Ex = xscope.enter_context
        xstage = Ex(nc.sbuf_tensor("xstage", [P, 2, DM], F32))
        xTb = Ex(nc.sbuf_tensor("xTb", [P, DG, TOK], BF16))

        evac_done = [0] * NTT
        for tt in range(NTT):
            @block.sync
            def _(sync, tt=tt):
                if tt >= 2:
                    sV.wait(sync, evac_done[tt - 2])
                sXD[tt % 2].inc(sync.dma_start(
                    out=xstage[:, tt % 2, :],
                    in_=x_h.ap()[tt * P:(tt + 1) * P, :]), 16)
            v_x = sXD[tt % 2].n

            for dg in range(DG):
                @block.tensor
                def _(tensor, tt=tt, dg=dg, v_x=v_x):
                    bk = 2 + (dg % 2)
                    bwait_free(tensor, bk)
                    if dg == 0:
                        sXD[tt % 2].wait(tensor, v_x)
                        if tt == 0:
                            sV.wait(tensor, v_setup)
                    mm = tensor.transpose(
                        pb[bk][:, 0:P], xstage[:, tt % 2, dg * P:(dg + 1) * P],
                        id_s[:])
                    bank_prod[bk] = (sT, sT.inc(mm))

                @bvec
                def _(vector, tt=tt, dg=dg):
                    bk = 2 + (dg % 2)
                    bwait_prod(vector, bk)
                    vector.tensor_copy(
                        out=xT[:, dg, tt * P:(tt + 1) * P], in_=pb[bk][:, 0:P])
                    cp = vector.tensor_copy(
                        out=xTb[:, dg, tt * P:(tt + 1) * P],
                        in_=pb[bk][:, 0:P])
                    bank_free[bk] = (sV, sV.inc(cp))
            evac_done[tt] = sV.n
        v_xready = sV.n

        for nt in range(TOK // N5):
            @block.tensor
            def _(tensor, nt=nt):
                if nt == 0:
                    sV.wait(tensor, v_xready)
                bk = 4 + (nt % 2)
                bwait_free(tensor, bk)
                for dg in range(DG):
                    mm = tensor.matmul(
                        pb[bk][0:96, 0:N5], wxtb_s[:, dg, :],
                        xTb[:, dg, nt * N5:(nt + 1) * N5],
                        start=(dg == 0), stop=(dg == DG - 1))
                bank_prod[bk] = (sT, sT.inc(mm))

            @bscal
            def _(scalar, nt=nt):
                bk = 4 + (nt % 2)
                bwait_prod(scalar, bk)
                cp = scalar.copy(out=proj_s[:, nt * N5:(nt + 1) * N5],
                                 in_=pb[bk][0:96, 0:N5])
                bank_free[bk] = (sA, sA.inc(cp))
        v_proj = sA.n

        @bvec
        def _(vector):
            sA.wait(vector, v_proj)
            sV.inc(vector.tensor_copy(out=projb_s[:], in_=proj_s[0:R, :]))
        v_projb = sV.n
        keep or xscope.close()

        SH = S // 2
        sscope = contextlib.ExitStack()
        Es = sscope.enter_context
        dtall_s = Es(nc.sbuf_tensor("dtall_s", [P, DG, TC], F32))
        aux_s = Es(nc.sbuf_tensor("aux_s", [P, DG, TC], F32))
        bbc_s = Es(nc.sbuf_tensor("bbc_s", [P, TC, SH], F32))
        cbc_s = Es(nc.sbuf_tensor("cbc_s", [P, TC, SH], F32))
        da_s = Es(nc.sbuf_tensor("da_s", [P, 2, TC, SH], F32))
        u_s = Es(nc.sbuf_tensor("u_s", [P, TC, SH], F32))
        hh_s = Es(nc.sbuf_tensor("hh_s", [P, TC, SH], F32))
        dt2_s = Es(nc.sbuf_tensor("dt2_s", [P, TC], F32))
        y_s = Es(nc.sbuf_tensor("y_s", [P, TC], F32))
        brow_s = Es(nc.sbuf_tensor("brow_s", [1, SH * TC], F32))
        crow_s = Es(nc.sbuf_tensor("crow_s", [1, SH * TC], F32))

        rows_waw = {"b": 0, "c": 0}

        def emit_rows(tc, sh, want_b, v_reader_done):
            t0 = tc * TC
            kinds = (["b"] if want_b else []) + ["c"]

            @block.sync
            def _(sync, tc=tc, sh=sh, kinds=tuple(kinds),
                  w_b=rows_waw["b"], w_c=rows_waw["c"]):
                sA.wait(sync, v_proj)
                for kind in kinds:
                    w = w_b if kind == "b" else w_c
                    if w:
                        sT.wait(sync, w)
                    base = 64 + sh * SH if kind == "b" else 80 + sh * SH
                    dst = brow_s if kind == "b" else crow_s
                    sR.inc(sync.dma_start(
                        out=dst[0:1, :],
                        in_=proj_s[base:base + SH, t0:t0 + TC]), 16)
            v_rows = sR.n

            for i, kind in enumerate(kinds):
                for si in range(SH):
                    bk = (i * SH + si) % 2
                    srcrow = brow_s if kind == "b" else crow_s

                    @block.tensor
                    def _(tensor, bk=bk, si=si, srcrow=srcrow, kind=kind,
                          first=(i == 0 and si == 0), v_rows=v_rows):
                        bwait_free(tensor, bk)
                        if first:
                            sR.wait(tensor, v_rows)
                        mm = tensor.matmul(
                            pb[bk][:, 0:TC], ones_all[0:1, :],
                            srcrow[0:1, si * TC:(si + 1) * TC],
                            start=True, stop=True)
                        bank_prod[bk] = (sT, sT.inc(mm))
                        rows_waw[kind] = sT.n

                    @bvec
                    def _(vector, bk=bk, si=si, kind=kind):
                        bwait_prod(vector, bk)
                        dst = bbc_s if kind == "b" else cbc_s
                        vector.tensor_copy(out=dst[:, :, si], in_=pb[bk][:, 0:TC])
                        bank_free[bk] = (sV, sV.n)
            return sV.n

        def emit_dt(tc, dg, v_waw_vec, phase, b, first):
            t0 = tc * TC

            @block.tensor
            def _(tensor):
                bwait_free(tensor, 2)
                if tc == 0 and dg == 0:
                    sV.wait(tensor, v_projb)
                mm = tensor.matmul(
                    pb[2][:, 0:TC], dtwtb_s[:, dg * P:(dg + 1) * P],
                    projb_s[:, t0:t0 + TC], start=True, stop=True)
                bank_prod[2] = (sT, sT.inc(mm))

            @bscal
            def _(scalar):
                bwait_prod(scalar, 2)
                if v_waw_vec:
                    sV.wait(scalar, v_waw_vec)
                bias = dtb_s[:, dg:dg + 1]
                scalar.activation(out=dt2_s[:], in_=pb[2][:, 0:TC],
                                  func=AF.Abs, bias=bias)
                scalar.activation(out=dt2_s[:], in_=dt2_s[:],
                                  func=AF.Exp, scale=-1.0)
                scalar.activation(out=dt2_s[:], in_=dt2_s[:],
                                  func=AF.Ln, bias=1.0)
                rl = scalar.activation(out=dtall_s[:, dg, :], in_=pb[2][:, 0:TC],
                                       func=AF.Relu, bias=bias)
                bank_free[2] = (sA, sA.inc(rl))
            v_sp = sA.n

            @bvec
            def _(vector, v_sp=v_sp, dg=dg, b=b, first=first, phase=phase,
                  v_waw_vec=v_waw_vec):
                if v_waw_vec:
                    vector.wait_ge(sV.h, v_waw_vec)
                sA.wait(vector, v_sp)
                dtv = dtall_s[:, dg, :]
                vector.tensor_add(dtv, dtv, dt2_s[:])
                if phase == "A":
                    vector.reduce_sum(out=red_s[:, 0:1], in_=dtv, axis=AX.X)
                    vector.tensor_add(ssum_s[:, b, dg:dg + 1],
                                      ssum_s[:, b, dg:dg + 1], red_s[:, 0:1])
                    f = vector.tensor_mul(aux_s[:, dg, :], dtv,
                                          xT[:, dg, t0:t0 + TC])
                else:
                    init = 0.0 if first else sstate_s[:, dg:dg + 1]
                    vector.tensor_tensor_scan(
                        out=aux_s[:, dg, :], data0=dtv, data1=zeros_s[:],
                        initial=init, op0=ALU.add, op1=ALU.add)
                    f = vector.tensor_copy(out=sstate_s[:, dg:dg + 1],
                                           in_=aux_s[:, dg, TC - 1:TC])
                sV.inc(f)
            return sV.n

        v_vec_done = 0
        for tc in range(NTC):
            b = tc // TCB
            first = (tc % TCB == 0)
            last = (tc % TCB == TCB - 1)
            t0 = tc * TC
            for dg in range(DG):
                emit_dt(tc, dg, v_vec_done, "A", b, first)
                v_vec_done = sV.n
            v_dtall = sV.n

            for sh in range(2):
                v_bc = emit_rows(tc, sh, True, v_vec_done)
                for dg in range(DG):
                    par = dg % 2

                    @bvec
                    def _(vector, dg=dg, sh=sh, par=par):
                        vector.tensor_tensor(
                            out=da_s[:, par],
                            in0=dtall_s[:, dg, :].unsqueeze(2).broadcast_to(
                                [P, TC, SH]),
                            in1=a_s[:, dg, sh * SH:(sh + 1) * SH].unsqueeze(
                                1).broadcast_to([P, TC, SH]),
                            op=ALU.mult)
                    v_arg = sV.n

                    @bscal
                    def _(scalar, par=par, v_arg=v_arg):
                        sV.wait(scalar, v_arg)
                        scalar.activation(out=da_s[:, par], in_=da_s[:, par],
                                          func=AF.Exp)
                    v_da = sA.n

                    @bvec
                    def _(vector, dg=dg, sh=sh, t0=t0, first=first, par=par,
                          v_da=v_da):
                        dtxb = aux_s[:, dg, :].unsqueeze(2).broadcast_to(
                            [P, TC, SH])
                        vector.tensor_tensor(out=u_s[:], in0=bbc_s[:],
                                             in1=dtxb, op=ALU.mult)
                        sA.wait(vector, v_da)
                        for si in range(SH):
                            sg = sh * SH + si
                            init = 0.0 if first else state_s[:, dg, sg:sg + 1]
                            vector.tensor_tensor_scan(
                                out=hh_s[:, :, si], data0=da_s[:, par, :, si],
                                data1=u_s[:, :, si], initial=init,
                                op0=ALU.mult, op1=ALU.add)
                        vector.tensor_copy(
                            out=state_s[:, dg, sh * SH:(sh + 1) * SH],
                            in_=hh_s[:, TC - 1, :])
                        vector.tensor_mul(hh_s[:], hh_s[:], cbc_s[:])
                        vector.reduce_sum(out=y_s[:], in_=hh_s[:], axis=AX.X)
                        if sh == 0:
                            f = vector.scalar_tensor_tensor(
                                out=xT[:, dg, t0:t0 + TC],
                                in0=xT[:, dg, t0:t0 + TC],
                                scalar=dp1_s[:, dg:dg + 1], in1=y_s[:],
                                op0=ALU.mult, op1=ALU.add)
                        else:
                            f = vector.tensor_add(
                                xT[:, dg, t0:t0 + TC],
                                xT[:, dg, t0:t0 + TC], y_s[:])
                        sV.inc(f)
                    v_vec_done = sV.n

            if last:
                @bvec
                def _(vector, b=b):
                    sV.inc(vector.tensor_copy(
                        out=hT_s[:, b, :],
                        in_=state_s[:].rearrange("p g s -> p (g s)")))
                v_vec_done = sV.n

        @bvec
        def _(vector, v_sw=sV.n):
            vector.wait_ge(sV.h, v_sw)
            vector.tensor_copy(out=pack_s[:, 0:P], in_=hT_s[:, 0, :])
            vector.tensor_copy(out=pack_s[:, P:2 * P], in_=hT_s[:, 1, :])
            sV.inc(vector.tensor_copy(
                out=pack_s[:, 2 * P:2 * P + 2 * DG],
                in_=ssum_s[:].rearrange("p b g -> p (b g)")))
        v_pack = sV.n

        @bgps
        def _(gpsimd):
            sV.wait(gpsimd, v_pack)
            sC.inc(gpsimd.dma_start(
                out=cci_h.ap().rearrange("(p c) -> p c", p=P),
                in_=pack_s[:]), 16)
            sC.wait(gpsimd)
            sCC.inc(gpsimd.collective_compute(
                "AllGather", ALU.bypass, replica_groups=[list(range(NCORE))],
                ins=[cci_h.ap()], outs=[cco_h.ap()]), 1)
            sCC.wait(gpsimd)
            sC.inc(gpsimd.dma_start(
                out=gath_s[:],
                in_=cco_h.ap().rearrange("(r p c) -> p r c", p=P, r=NCORE)), 16)
        v_gath = sC.n

        for b in range(NB):
            for j in range(NCORE - 1):
                @bvec
                def _(vector, b=b, j=j):
                    if b == 0 and j == 0:
                        sC.wait(vector, v_gath)
                    vector.tensor_scalar_mul(
                        tdg_s[:],
                        gath_s[:, j, 2 * P + b * DG:2 * P + (b + 1) * DG],
                        mask_s[:, j:j + 1])
                    smb = tdg_s[:].unsqueeze(2).broadcast_to([P, DG, S])
                    sV.inc(vector.tensor_tensor(out=edg_s[:], in0=a_s[:],
                                                in1=smb, op=ALU.mult))
                v_1 = sV.n

                @bscal
                def _(scalar, v_1=v_1):
                    sV.wait(scalar, v_1)
                    sA.inc(scalar.activation(out=edg_s[:], in_=edg_s[:],
                                             func=AF.Exp))
                v_2 = sA.n

                @bvec
                def _(vector, b=b, j=j, v_2=v_2, v_sw=sV.n):
                    vector.wait_ge(sV.h, v_sw)
                    sA.wait(vector, v_2)
                    hv = hin_s[:, b, :, :].rearrange("p g s -> p (g s)")
                    ev = edg_s[:].rearrange("p g s -> p (g s)")
                    vector.tensor_mul(hv, hv, ev)
                    vector.tensor_scalar_mul(
                        thp_s[:], gath_s[:, j, b * P:(b + 1) * P],
                        mask_s[:, j:j + 1])
                    sV.inc(vector.tensor_add(hv, hv, thp_s[:]))
        v_vec_done = sV.n

        for tc in range(NTC):
            b = tc // TCB
            first = (tc % TCB == 0)
            t0 = tc * TC
            for dg in range(DG):
                emit_dt(tc, dg, v_vec_done, "B", b, first)
                v_vec_done = sV.n
            v_sall = sV.n

            for sh in range(2):
                v_bc = emit_rows(tc, sh, False, v_vec_done)
                for dg in range(DG):
                    par = dg % 2

                    @bvec
                    def _(vector, dg=dg, sh=sh, par=par):
                        vector.tensor_tensor(
                            out=da_s[:, par],
                            in0=aux_s[:, dg, :].unsqueeze(2).broadcast_to(
                                [P, TC, SH]),
                            in1=a_s[:, dg, sh * SH:(sh + 1) * SH].unsqueeze(
                                1).broadcast_to([P, TC, SH]),
                            op=ALU.mult)
                    v_arg = sV.n

                    @bscal
                    def _(scalar, par=par, v_arg=v_arg):
                        sV.wait(scalar, v_arg)
                        scalar.activation(out=da_s[:, par], in_=da_s[:, par],
                                          func=AF.Exp)
                    v_e = sA.n

                    @bvec
                    def _(vector, b=b, dg=dg, sh=sh, t0=t0, par=par,
                          v_e=v_e):
                        sA.wait(vector, v_e)
                        for si in range(SH):
                            sg = sh * SH + si
                            vector.scalar_tensor_tensor(
                                out=u_s[:, :, si], in0=da_s[:, par, :, si],
                                scalar=hin_s[:, b, dg, sg:sg + 1],
                                in1=cbc_s[:, :, si], op0=ALU.mult, op1=ALU.mult)
                        vector.reduce_sum(out=y_s[:], in_=u_s[:], axis=AX.X)
                        sV.inc(vector.tensor_add(
                            xT[:, dg, t0:t0 + TC], xT[:, dg, t0:t0 + TC],
                            y_s[:]))
                    v_vec_done = sV.n
        v_phaseB_v = sV.n

        keep or sscope.close()
        keep or mscope.close()

        ln1b_s = E(nc.sbuf_tensor("ln1b_s", [P, DG, TOK], BF16, side="right"))
        acc_s = E(nc.sbuf_tensor("acc_s", [P, NTT, DM], F32, side="right"))

        lscope = contextlib.ExitStack()
        El = lscope.enter_context
        mrow_s = El(nc.sbuf_tensor("mrow_s", [1, TC], F32))
        rrow_s = El(nc.sbuf_tensor("rrow_s", [1, TC], F32))
        trow_s = El(nc.sbuf_tensor("trow_s", [1, TC], F32))
        sq_s = El(nc.sbuf_tensor("sq_s", [P, 2, TC], F32))

        v_rstdt = [0] * NTC
        sq_mm = [0, 0]
        v_bcast_prev = 0
        v_var_prev = 0
        for tc in range(NTC):
            t0 = tc * TC
            for dg in range(DG):
                @bvec
                def _(vector, dg=dg, t0=t0):
                    if dg >= 2:
                        sT.wait(vector, sq_mm[dg % 2])
                    vector.tensor_mul(sq_s[:, dg % 2, :],
                                      xT[:, dg, t0:t0 + TC],
                                      xT[:, dg, t0:t0 + TC])
                v_sq = sV.n

                @block.tensor
                def _(tensor, dg=dg, t0=t0, v_sq=v_sq):
                    sV.wait(tensor, v_sq)
                    if dg == 0:
                        bwait_free(tensor, 6)
                        bwait_free(tensor, 7)
                    tensor.matmul(pb[6][0:1, 0:TC], ones_all[:, 0:1],
                                  xT[:, dg, t0:t0 + TC],
                                  start=(dg == 0), stop=(dg == DG - 1))
                    mm = tensor.matmul(pb[7][0:1, 0:TC], ones_all[:, 0:1],
                                       sq_s[:, dg % 2, :],
                                       start=(dg == 0), stop=(dg == DG - 1))
                    sq_mm[dg % 2] = sT.inc(mm)
            v_stats = sT.n
            bank_prod[6] = (sT, v_stats)
            bank_prod[7] = (sT, v_stats)

            @bvec
            def _(vector, tc=tc, v_stats=v_stats, vb=v_bcast_prev):
                sT.wait(vector, v_stats)
                if tc > 0:
                    sT.wait(vector, vb)
                    sRS.wait(vector, v_rstdt[tc - 1])
                vector.tensor_scalar_mul(mrow_s[:], pb[6][0:1, 0:TC], 1.0 / DM)
                vector.tensor_scalar_mul(trow_s[:], pb[7][0:1, 0:TC], 1.0 / DM)
                vector.tensor_mul(rrow_s[:], mrow_s[:], mrow_s[:])
                vector.tensor_sub(trow_s[:], trow_s[:], rrow_s[:])
            v_var = sV.n
            bank_free[6] = (sV, v_var)
            bank_free[7] = (sV, v_var)

            @bscal
            def _(scalar, v_var=v_var):
                sV.wait(scalar, v_var)
                scalar.activation(out=trow_s[:], in_=trow_s[:],
                                  func=AF.Ln, bias=eps_s[0:1, 0:1])
                sA.inc(scalar.activation(out=rrow_s[:], in_=trow_s[:],
                                         func=AF.Exp, scale=-0.5))
            v_rstd = sA.n

            @block.tensor
            def _(tensor, tc=tc, v_rstd=v_rstd):
                sA.wait(tensor, v_rstd)
                bwait_free(tensor, 4)
                bwait_free(tensor, 5)
                tensor.matmul(pb[4][:, 0:TC], ones_all[0:1, :], mrow_s[:],
                              start=True, stop=True)
                mm = tensor.matmul(pb[5][:, 0:TC], ones_all[0:1, :], rrow_s[:],
                                   start=True, stop=True)
                sT.inc(mm)
            v_bcast = sT.n
            v_bcast_prev = v_bcast
            bank_prod[4] = (sT, v_bcast)
            bank_prod[5] = (sT, v_bcast)

            for dg in range(DG):
                @bvec
                def _(vector, dg=dg, t0=t0, v_bcast=v_bcast):
                    sT.wait(vector, v_bcast)
                    vector.tensor_sub(xT[:, dg, t0:t0 + TC],
                                      xT[:, dg, t0:t0 + TC], pb[4][:, 0:TC])
                    sV.inc(vector.tensor_tensor(
                        out=ln1b_s[:, dg, t0:t0 + TC],
                        in0=xT[:, dg, t0:t0 + TC], in1=pb[5][:, 0:TC],
                        op=ALU.mult))
            v_apply = sV.n
            bank_free[4] = (sV, v_apply)
            bank_free[5] = (sV, v_apply)

            @block.sync
            def _(sync, tc=tc, v_rstd=v_rstd):
                sA.wait(sync, v_rstd)
                sRS.inc(sync.dma_start(
                    out=rstd_d.ap()[tc * TC:(tc + 1) * TC].unsqueeze(0),
                    in_=rrow_s[0:1, :]), 16)
            v_rstdt[tc] = sRS.n
        v_ln1_v = sV.n

        @block.sync
        def _(sync):
            sRS.wait(sync, v_rstdt[NTC - 1])
            sRS.inc(sync.dma_start(
                out=rstdt_s[:],
                in_=rstd_d.ap().rearrange("(t p) -> p t", p=P),
                allow_slow_non_contiguous=True), 16)
        v_rstdt_load = sRS.n

        for tt in range(NTT):
            tc = (tt * P) // TC
            for dg in range(DG):
                @block.tensor
                def _(tensor, tt=tt, dg=dg):
                    bk = 2 + (dg % 2)
                    bwait_free(tensor, bk)
                    if tt == 0 and dg == 0:
                        sV.wait(tensor, v_ln1_v)
                    mm = tensor.transpose(pb[bk][:, 0:P],
                                          xT[:, dg, tt * P:(tt + 1) * P],
                                          id_s[:])
                    bank_prod[bk] = (sT, sT.inc(mm))

                @bvec
                def _(vector, tt=tt, dg=dg):
                    bk = 2 + (dg % 2)
                    bwait_prod(vector, bk)
                    if tt == 0 and dg == 0:
                        sRS.wait(vector, v_rstdt_load)
                    vector.tensor_single_scalar(
                        out=acc_s[:, tt, dg * P:(dg + 1) * P],
                        in_=pb[bk][:, 0:P], scalar=rstdt_s[:, tt:tt + 1],
                        op=ALU.mult)
                    bank_free[bk] = (sV, sV.n)

            @block.tensor
            def _(tensor, tt=tt):
                bwait_free(tensor, 6)
                for dg in range(DG):
                    tensor.matmul(pb[6][:, 0:NE], ln1b_s[:, dg, tt * P:(tt + 1) * P],
                                  wrb_s[:, dg, :], start=(dg == 0), stop=False)
                mm = tensor.matmul(pb[6][:, 0:NE], ones_all[0:1, :], br_s[:],
                                   start=False, stop=True)
                bank_prod[6] = (sT, sT.inc(mm))

            @bvec
            def _(vector):
                bwait_prod(vector, 6)
                sV.inc(vector.reduce_max(out=red_s[:, 0:1], in_=pb[6][:, 0:NE],
                                         axis=AX.X, negate=True))
            v_nm = sV.n

            @bscal
            def _(scalar, v_nm=v_nm):
                sV.wait(scalar, v_nm)
                e = scalar.activation(out=rscr_s[:, 0:NE], in_=pb[6][:, 0:NE],
                                      func=AF.Exp, bias=red_s[:, 0:1])
                bank_free[6] = (sA, sA.inc(e))
            v_e = sA.n

            @bvec
            def _(vector, tt=tt, v_e=v_e):
                sA.wait(vector, v_e)
                e = rscr_s[:, 0:NE]
                f = rscr_s[:, 8:8 + NE]
                w2 = rscr_s[:, 16:16 + NE]
                vector.reduce_max(out=red_s[:, 1:2], in_=e, axis=AX.X)
                vector.tensor_single_scalar(out=f, in_=e,
                                            scalar=red_s[:, 1:2],
                                            op=ALU.is_equal)
                vector.tensor_scalar(out=f, in0=f, scalar1=-1.0, scalar2=1.0,
                                     op0=ALU.mult, op1=ALU.add)
                vector.tensor_mul(w2, e, f)
                vector.reduce_max(out=red_s[:, 2:3], in_=w2, axis=AX.X)
                vector.tensor_single_scalar(out=f, in_=e,
                                            scalar=red_s[:, 2:3],
                                            op=ALU.is_ge)
                vector.tensor_mul(e, e, f)
                vector.reduce_sum(out=red_s[:, 3:4], in_=e, axis=AX.X)
                vector.reciprocal(out=red_s[:, 4:5], in_=red_s[:, 3:4])
                sV.inc(vector.tensor_scalar_mul(rtok_s[:, tt, :], e,
                                                red_s[:, 4:5]))
            v_rt = sV.n

            @block.tensor
            def _(tensor, tt=tt, v_rt=v_rt):
                bwait_free(tensor, 7)
                sV.wait(tensor, v_rt)
                mm = tensor.transpose(pb[7][0:NE, 0:P], rtok_s[:, tt, :],
                                      id_s[:])
                bank_prod[7] = (sT, sT.inc(mm))

            @bscal
            def _(scalar, tt=tt):
                bwait_prod(scalar, 7)
                cp = scalar.copy(out=rfm_s[:, tt * P:(tt + 1) * P],
                                 in_=pb[7][0:NE, 0:P])
                bank_free[7] = (sA, sA.inc(cp))
        v_accinit = sA.n
        v_rt_all = sV.n

        keep or lscope.close()
        keep or xT_scope.close()

        fscope = contextlib.ExitStack()
        Ef = fscope.enter_context
        stg = Ef(nc.sbuf_tensor("stg", [P, HQ], F32))
        w1qb = Ef(nc.sbuf_tensor("w1qb", [P, 2, DG, HQ], BF16))
        w2qb = Ef(nc.sbuf_tensor("w2qb", [P, 2, HC, DM], BF16))
        h_sb = Ef(nc.sbuf_tensor("h_sb", [P, HC, N5], BF16))
        out_s = Ef(nc.sbuf_tensor("out_s", [P, 2, DM], F32))

        NQT = NE * NQ
        W1B = (0, 1, 4, 5)
        EOB = (2, 3, 6, 7)
        v_w1_mmdone = {}
        v_w2_mmdone = {}
        v_castdone = {}
        v_b1 = {}
        v_b1_use = {}
        heread = {0: 0, 1: 0}
        cast_prev = [0]

        def emit_stream(QI):
            e, q = divmod(QI, NQ)
            slot = QI % 2
            if q == 0:
                @block.sync
                def _(sync, e=e, v=v_b1_use.get(e - 2, 0)):
                    if v:
                        sA.wait(sync, v)
                    sB1.inc(sync.dma_start(out=b1r_s[:, e % 2, :],
                                           in_=b1_h.ap()[e]), 16)
                v_b1[e] = sB1.n

            vfree = (v_w1_mmdone.get(QI - 2, 0), v_w2_mmdone.get(QI - 2, 0))
            for half in range(2):
                for ck in range(DG):
                    @block.sync
                    def _(sync, e=e, q=q, ck=ck, half=half, QI=QI,
                          vc=cast_prev[0]):
                        if QI == 0 and half == 0 and ck == 0:
                            sA.wait(sync, v_accinit)
                        if vc:
                            sG.wait(sync, vc)
                        if half == 0:
                            src_ap = w1_h.ap()[
                                e, ck * P:(ck + 1) * P,
                                q * HQ:(q + 1) * HQ]
                        else:
                            src_ap = w2_h.ap()[
                                e, q * HQ + ck * P:q * HQ + (ck + 1) * P, :]
                        sW1.inc(sync.dma_start(out=stg[:], in_=src_ap), 16)
                    v_dma = sW1.n

                    @bgps
                    def _(gpsimd, slot=slot, ck=ck, half=half, v_dma=v_dma,
                          vfree=vfree[half], first=(ck == 0)):
                        sW1.wait(gpsimd, v_dma)
                        if first and vfree:
                            sT.wait(gpsimd, vfree)
                        dst = (w1qb if half == 0 else w2qb)
                        gpsimd.tensor_copy(
                            out=dst[:, slot, ck, :], in_=stg[:])
                        cast_prev[0] = sG.n
            v_castdone[QI] = sG.n

        def emit_compute(QI):
            e, q = divmod(QI, NQ)
            slot = QI % 2
            b2eff = (e == 0 and q == 0)
            for t5 in range(NT5):
                t0 = t5 * N5
                gW = [0] * HC

                @block.tensor
                def _(tensor, e=e, q=q, QI=QI, t5=t5, slot=slot, t0=t0):
                    if t5 == 0:
                        sG.wait(tensor, v_castdone[QI])
                        if QI == 0:
                            sV.wait(tensor, v_ln1_v)
                    vrelu_base = sA.n
                    for hdc in range(HC):
                        bk = W1B[hdc % 4]
                        for dg in range(DG):
                            mm = tensor.matmul(
                                pb[bk][:, 0:N5],
                                w1qb[:, slot, dg, hdc * P:(hdc + 1) * P],
                                ln1b_s[:, dg, t0:t0 + N5],
                                start=(dg == 0), stop=(dg == DG - 1))
                            if dg == 0:
                                if hdc < 4:
                                    if bank_free[bk] is not None:
                                        c, v = bank_free[bk]
                                        mm._wait_ge(c.h, v)
                                else:
                                    mm._wait_ge(sA.h,
                                                vrelu_base + (hdc - 4) + 1)
                        gW[hdc] = sT.inc(mm)

                @bscal
                def _(scalar, e=e, q=q, vh=heread[0]):
                    for hdc in range(HC):
                        bk = W1B[hdc % 4]
                        if hdc == 0:
                            sB1.wait(scalar, v_b1[e])
                            if vh:
                                sT.wait(scalar, vh)
                        sT.wait(scalar, gW[hdc])
                        scalar.activation(
                            out=h_sb[:, hdc, :], in_=pb[bk][:, 0:N5],
                            func=AF.Relu,
                            bias=b1r_s[:, e % 2,
                                       q * HC + hdc:q * HC + hdc + 1])
                        bank_free[bk] = (sA, sA.n)
                v_hevac = sA.n
                v_w1_mmdone[QI] = sT.n
                mEO = [0] * (2 * T128)

                @block.tensor
                def _(tensor, e=e, QI=QI, t5=t5, slot=slot,
                      b2eff=b2eff, v_hevac=v_hevac):
                    sA.wait(tensor, v_hevac)
                    vstt_base = sV.n
                    for i in range(2 * T128):
                        t1, dh = divmod(i, 2)
                        ttg = t5 * T128 + t1
                        bk = EOB[i % 4]
                        for hdc in range(HC):
                            mm = tensor.matmul(
                                pb[bk][:, 0:512],
                                h_sb[:, hdc, t1 * P:(t1 + 1) * P],
                                w2qb[:, slot, hdc, dh * 512:(dh + 1) * 512],
                                start=(hdc == 0),
                                stop=(hdc == HC - 1 and not b2eff))
                            if hdc == 0:
                                if i < 4:
                                    if bank_free[bk] is not None:
                                        c, v = bank_free[bk]
                                        mm._wait_ge(c.h, v)
                                else:
                                    mm._wait_ge(sV.h,
                                                vstt_base + (i - 4) + 1)
                        if b2eff:
                            mm = tensor.matmul(
                                pb[bk][:, 0:512],
                                rfm_s[:, ttg * P:(ttg + 1) * P],
                                b2_s[:, dh * 512:(dh + 1) * 512],
                                start=False, stop=True)
                        mEO[i] = sT.inc(mm)
                heread[0] = sT.n
                v_w2_mmdone[QI] = sT.n

                @bvec
                def _(vector, e=e, QI=QI, t5=t5):
                    for i in range(2 * T128):
                        t1, dh = divmod(i, 2)
                        ttg = t5 * T128 + t1
                        bk = EOB[i % 4]
                        if i == 0 and QI == 0 and t5 == 0:
                            sA.wait(vector, v_accinit)
                            vector.wait_ge(sV.h, v_rt_all)
                        sT.wait(vector, mEO[i])
                        vector.scalar_tensor_tensor(
                            out=acc_s[:, ttg, dh * 512:(dh + 1) * 512],
                            in0=pb[bk][:, 0:512],
                            scalar=rtok_s[:, ttg, e:e + 1],
                            in1=acc_s[:, ttg, dh * 512:(dh + 1) * 512],
                            op0=ALU.mult, op1=ALU.add)
                        bank_free[bk] = (sV, sV.n)
            v_b1_use[e] = v_hevac

        emit_stream(0)
        for QI in range(NQT):
            if QI + 1 < NQT:
                emit_stream(QI + 1)
            emit_compute(QI)

        v_ffn_v = sV.n

        v_outdma = [0] * NTT
        for tt in range(NTT):
            @bvec
            def _(vector, tt=tt):
                if tt == 0:
                    sV.wait(vector, v_ffn_v)
                vector.reduce_sum(out=red_s[:, 0:1], in_=acc_s[:, tt, :],
                                  axis=AX.X)
                vector.tensor_scalar_mul(red_s[:, 1:2], red_s[:, 0:1],
                                         1.0 / DM)
                if tt >= 2:
                    sO[tt % 2].wait(vector, v_outdma[tt - 2])
                vector.scalar_tensor_tensor(
                    out=out_s[:, tt % 2, :], in0=acc_s[:, tt, :], scalar=1.0,
                    in1=acc_s[:, tt, :], op0=ALU.mult, op1=ALU.mult,
                    accum_out=red_s[:, 2:3])
                vector.tensor_scalar_mul(red_s[:, 3:4], red_s[:, 2:3],
                                         1.0 / DM)
                vector.tensor_mul(red_s[:, 4:5], red_s[:, 1:2], red_s[:, 1:2])
                sV.inc(vector.tensor_sub(red_s[:, 5:6], red_s[:, 3:4],
                                         red_s[:, 4:5]))
            v_var2 = sV.n

            @bscal
            def _(scalar, v_var2=v_var2):
                sV.wait(scalar, v_var2)
                scalar.activation(out=red_s[:, 6:7], in_=red_s[:, 5:6],
                                  func=AF.Ln, bias=eps_s[:, 0:1])
                sA.inc(scalar.activation(out=red_s[:, 6:7], in_=red_s[:, 6:7],
                                         func=AF.Exp, scale=-0.5))
            v_rstd2 = sA.n

            @bvec
            def _(vector, v_rstd2=v_rstd2):
                sA.wait(vector, v_rstd2)
                sV.inc(vector.scalar_tensor_tensor(
                    out=red_s[:, 7:8], in0=red_s[:, 1:2], scalar=-1.0,
                    in1=red_s[:, 6:7], op0=ALU.mult, op1=ALU.mult))
            v_nmr = sV.n

            @bvec
            def _(vector, tt=tt, v_nmr=v_nmr):
                vector.wait_ge(sV.h, v_nmr)
                nmb = red_s[:, 7:8].broadcast_to([P, DM])
                sV.inc(vector.scalar_tensor_tensor(
                    out=out_s[:, tt % 2, :], in0=acc_s[:, tt, :],
                    scalar=red_s[:, 6:7], in1=nmb,
                    op0=ALU.mult, op1=ALU.add))
            v_oev = sV.n

            @bgps
            def _(gpsimd, tt=tt, v_oev=v_oev):
                sV.wait(gpsimd, v_oev)
                sO[tt % 2].inc(gpsimd.dma_start(
                    out=out_h.ap()[tt * P:(tt + 1) * P, :],
                    in_=out_s[:, tt % 2, :]), 16)
            v_outdma[tt] = sO[tt % 2].n

        @bgps
        def _(gpsimd):
            sO[0].wait(gpsimd)
            sO[1].wait(gpsimd)

        keep or fscope.close()
        if keep:
            for sc in (fscope, lscope, sscope, xscope, mscope, xT_scope):
                sc.close()

    return nc


def prepare_in_maps(L0_attn_x, L0_attn_A_log, L0_attn_D, L0_attn_Wx,
                    L0_attn_dt_W, L0_attn_dt_b, L0_ffn_W1, L0_ffn_b1,
                    L0_ffn_W2, L0_ffn_b2, L0_ffn_Wr, L0_ffn_br):
    CL = L0_attn_x.shape[1] // NCORE
    x = np.ascontiguousarray(L0_attn_x, dtype=np.float32)
    c = np.ascontiguousarray

    def pd(v):
        v = np.asarray(v, dtype=np.float32)
        return c(v.reshape(DG, P, *v.shape[1:]).transpose(
            1, 0, *range(2, v.ndim + 1)))

    shared = {
        "WxT": c(np.asarray(L0_attn_Wx, np.float32).T),
        "dtWT": c(np.asarray(L0_attn_dt_W, np.float32).T),
        "A_pd": pd(L0_attn_A_log),
        "D_pd": pd(L0_attn_D),
        "dtb_pd": pd(L0_attn_dt_b),
        "ident": np.eye(P, dtype=np.float32),
        "Wr": c(np.asarray(L0_ffn_Wr, np.float32)),
        "br_r": c(np.asarray(L0_ffn_br, np.float32).reshape(1, NE)),
        "b2": c(np.asarray(L0_ffn_b2, np.float32)),
        "b1r": c(np.asarray(L0_ffn_b1, np.float32).reshape(
            NE, H // P, P).transpose(0, 2, 1)),
        "W1": c(np.asarray(L0_ffn_W1, np.float32)),
        "W2": c(np.asarray(L0_ffn_W2, np.float32)),
    }

    in_maps = []
    for cid in range(NCORE):
        m = dict(shared)
        m["x01"] = c(np.concatenate(
            [x[0, cid * CL:(cid + 1) * CL], x[1, cid * CL:(cid + 1) * CL]],
            axis=0))
        m["maskv"] = c(np.broadcast_to(
            (np.arange(NCORE) < cid).astype(np.float32), (P, NCORE)))
        in_maps.append(m)
    return in_maps, CL


def assemble_out(results, CL):
    out = np.empty((NB, NCORE * CL, DM), dtype=np.float32)
    for cid in range(NCORE):
        o = results[cid]["out"]
        out[0, cid * CL:(cid + 1) * CL] = o[:CL]
        out[1, cid * CL:(cid + 1) * CL] = o[CL:]
    return out


def kernel(L0_attn_x, L0_attn_A_log, L0_attn_D, L0_attn_Wx, L0_attn_dt_W,
           L0_attn_dt_b, L0_ffn_W1, L0_ffn_b1, L0_ffn_W2, L0_ffn_b2,
           L0_ffn_Wr, L0_ffn_br):
    in_maps, CL = prepare_in_maps(
        L0_attn_x, L0_attn_A_log, L0_attn_D, L0_attn_Wx, L0_attn_dt_W,
        L0_attn_dt_b, L0_ffn_W1, L0_ffn_b1, L0_ffn_W2, L0_ffn_b2,
        L0_ffn_Wr, L0_ffn_br)
    nc = build_nc(CL=CL)
    res = run_bass_kernel_spmd(nc, in_maps, core_ids=list(range(NCORE)))
    return assemble_out(res.results, CL)


# version 08
# speedup vs baseline: 1.0317x; 1.0317x over previous
import contextlib
import numpy as np
import concourse.bass as bass
import concourse.mybir as mybir
from concourse.bass_utils import run_bass_kernel_spmd

F32 = mybir.dt.float32
BF16 = mybir.dt.bfloat16
AF = mybir.ActivationFunctionType
ALU = mybir.AluOpType
AX = mybir.AxisListType

P = 128
DM = 1024
DG = 8
S = 16
R = 64
NB = 2
NCORE = 8
H = 4096
NE = 8
NQ = 4
HQ = H // NQ
HC = HQ // P
EPS = 1e-5


class Ctr:
    def __init__(self, ctx, nc, name):
        self.h = ctx.enter_context(nc.semaphore(name))
        self.n = 0
        self.seen = {}

    def inc(self, instr, k=1):
        if id(instr) in self.seen:
            return self.n
        instr.then_inc(self.h, k)
        self.n += k
        return self.n

    def wait(self, eng, at=None):
        eng.wait_ge(self.h, self.n if at is None else at)


class ChainEng:
    _PASS = frozenset(("wait_ge", "drain"))
    _NOAUTO = frozenset(("dma_start", "collective_compute"))

    def __init__(self, eng, ctr):
        self._e = eng
        self._c = ctr

    def __getattr__(self, name):
        f = getattr(self._e, name)
        if name in ChainEng._PASS or name in ChainEng._NOAUTO \
                or not callable(f):
            return f
        c = self._c

        def g(*a, **k):
            n0 = c.n
            ins = f(*a, **k)
            if n0:
                ins._wait_ge(c.h, n0)
            c.inc(ins)
            c.seen[id(ins)] = ins
            return ins
        return g


def chained_block(deco, ctr):
    def d(fn):
        def inner(eng):
            return fn(ChainEng(eng, ctr))
        return deco(inner)
    return d


def build_nc(CL=1024, TC=256, keep=False):
    TOK = NB * CL
    NTC = TOK // TC
    TCB = CL // TC
    NTT = TOK // P
    N5 = min(512, TOK)
    NT5 = TOK // N5
    T128 = N5 // P
    PACKC = 2 * P + 2 * DG

    nc = bass.Bass("TRN2", target_bir_lowering=False)

    x_h = nc.dram_tensor("x01", [TOK, DM], F32, kind="ExternalInput")
    wxt_h = nc.dram_tensor("WxT", [DM, 96], F32, kind="ExternalInput")
    dtwt_h = nc.dram_tensor("dtWT", [R, DM], F32, kind="ExternalInput")
    a_h = nc.dram_tensor("A_pd", [P, DG, S], F32, kind="ExternalInput")
    d_h = nc.dram_tensor("D_pd", [P, DG], F32, kind="ExternalInput")
    dtb_h = nc.dram_tensor("dtb_pd", [P, DG], F32, kind="ExternalInput")
    mask_h = nc.dram_tensor("maskv", [P, NCORE], F32, kind="ExternalInput")
    id_h = nc.dram_tensor("ident", [P, P], F32, kind="ExternalInput")
    wr_h = nc.dram_tensor("Wr", [DM, NE], F32, kind="ExternalInput")
    br_h = nc.dram_tensor("br_r", [1, NE], F32, kind="ExternalInput")
    b2_h = nc.dram_tensor("b2", [NE, DM], F32, kind="ExternalInput")
    b1_h = nc.dram_tensor("b1r", [NE, P, H // P], F32, kind="ExternalInput")
    w1_h = nc.dram_tensor("W1", [NE, DM, H], F32, kind="ExternalInput")
    w2_h = nc.dram_tensor("W2", [NE, H, DM], F32, kind="ExternalInput")
    out_h = nc.dram_tensor("out", [TOK, DM], F32, kind="ExternalOutput")
    cci_h = nc.dram_tensor("cc_in", [P * PACKC], F32, kind="Internal")
    rstd_d = nc.dram_tensor("rstd_d", [NB * CL], F32, kind="Internal")
    cco_h = nc.dram_tensor("cc_out", [NCORE * P * PACKC], F32, kind="Internal")

    ctx = contextlib.ExitStack()
    with ctx:
        E = ctx.enter_context

        sX = Ctr(ctx, nc, "sX")
        sXD = [Ctr(ctx, nc, "sXD0"), Ctr(ctx, nc, "sXD1")]
        sR = Ctr(ctx, nc, "sR")
        sB1 = Ctr(ctx, nc, "sB1")
        sRS = Ctr(ctx, nc, "sRS")
        sW1 = Ctr(ctx, nc, "sW1")
        sW2 = Ctr(ctx, nc, "sW2")
        sT = Ctr(ctx, nc, "sT")
        sA = Ctr(ctx, nc, "sA")
        sV = Ctr(ctx, nc, "sV")
        sG = Ctr(ctx, nc, "sG")
        sC = Ctr(ctx, nc, "sC")
        sCC = Ctr(ctx, nc, "sCC")
        sO = [Ctr(ctx, nc, "sO0"), Ctr(ctx, nc, "sO1")]

        id_s = E(nc.sbuf_tensor("id_s", [P, P], F32))
        ones_all = E(nc.sbuf_tensor("ones_all", [P, P], F32))
        a_s = E(nc.sbuf_tensor("a_s", [P, DG, S], F32))
        dp1_s = E(nc.sbuf_tensor("dp1_s", [P, DG], F32))
        dtb_s = E(nc.sbuf_tensor("dtb_s", [P, DG], F32))
        mask_s = E(nc.sbuf_tensor("mask_s", [P, NCORE], F32))
        wxtb_s = E(nc.sbuf_tensor("wxtb_s", [P, DG, 96], BF16))
        dtwtb_s = E(nc.sbuf_tensor("dtwtb_s", [R, DM], BF16))
        wrb_s = E(nc.sbuf_tensor("wrb_s", [P, DG, NE], BF16))
        br_s = E(nc.sbuf_tensor("br_s", [1, NE], F32))
        b2_s = E(nc.sbuf_tensor("b2_s", [NE, DM], F32))
        b1r_s = E(nc.sbuf_tensor("b1r_s", [P, 2, H // P], F32))
        zeros_s = E(nc.sbuf_tensor("zeros_s", [P, TC], F32))
        state_s = E(nc.sbuf_tensor("state_s", [P, DG, S], F32))
        sstate_s = E(nc.sbuf_tensor("sstate_s", [P, DG], F32))
        ssum_s = E(nc.sbuf_tensor("ssum_s", [P, NB, DG], F32))
        hT_s = E(nc.sbuf_tensor("hT_s", [P, NB, P], F32))
        hin_s = E(nc.sbuf_tensor("hin_s", [P, NB, DG, S], F32))
        red_s = E(nc.sbuf_tensor("red_s", [P, 8], F32))
        rscr_s = E(nc.sbuf_tensor("rscr_s", [P, 24], F32))
        rfm_s = E(nc.sbuf_tensor("rfm_s", [NE, TOK], F32))
        rtok_s = E(nc.sbuf_tensor("rtok_s", [P, NTT, NE], F32))
        rstdt_s = E(nc.sbuf_tensor("rstdt_s", [P, NTT], F32))
        edg_s = E(nc.sbuf_tensor("edg_s", [P, DG, S], F32))
        tdg_s = E(nc.sbuf_tensor("tdg_s", [P, DG], F32))
        thp_s = E(nc.sbuf_tensor("thp_s", [P, P], F32))
        eps_s = E(nc.sbuf_tensor("eps_s", [P, 1], F32))

        pb = [E(nc.psum_tensor(f"pb{i}", [P, 512], F32)) for i in range(8)]
        bank_free = [None] * 8
        bank_prod = [None] * 8

        def bwait_free(eng, bk):
            if bank_free[bk] is not None:
                c, v = bank_free[bk]
                c.wait(eng, v)

        def bwait_prod(eng, bk):
            c, v = bank_prod[bk]
            c.wait(eng, v)

        block = E(nc.Block())
        bvec = chained_block(block.vector, sV)
        bscal = chained_block(block.scalar, sA)
        bgps = chained_block(block.gpsimd, sG)

        xT_scope = contextlib.ExitStack()
        xT = xT_scope.enter_context(nc.sbuf_tensor("xT", [P, DG, TOK], F32))

        mscope = contextlib.ExitStack()
        Em = mscope.enter_context
        wxts_s = Em(nc.sbuf_tensor("wxts_s", [P, DG, 96], F32))
        dtwts_s = Em(nc.sbuf_tensor("dtwts_s", [R, DM], F32))
        wrs_s = Em(nc.sbuf_tensor("wrs_s", [P, DG, NE], F32))
        proj_s = Em(nc.sbuf_tensor("proj_s", [96, TOK], F32))
        projb_s = Em(nc.sbuf_tensor("projb_s", [R, TOK], BF16))
        gath_s = Em(nc.sbuf_tensor("gath_s", [P, NCORE, PACKC], F32))
        pack_s = Em(nc.sbuf_tensor("pack_s", [P, PACKC], F32))

        @block.sync
        def _(sync):
            for dst, src in ((id_s, id_h), (a_s, a_h), (dp1_s, d_h),
                             (dtb_s, dtb_h), (mask_s, mask_h), (br_s, br_h),
                             (b2_s, b2_h)):
                sX.inc(sync.dma_start(out=dst[:], in_=src.ap()), 16)
            sX.inc(sync.dma_start(
                out=wxts_s[:],
                in_=wxt_h.ap().rearrange("(g p) m -> p g m", p=P)), 16)
            sX.inc(sync.dma_start(out=dtwts_s[:], in_=dtwt_h.ap()), 16)
            sX.inc(sync.dma_start(
                out=wrs_s[:],
                in_=wr_h.ap().rearrange("(g p) e -> p g e", p=P)), 16)
        v_const = sX.n

        @bscal
        def _(scalar):
            sX.wait(scalar, v_const)
            sA.inc(scalar.activation(out=a_s[:], in_=a_s[:], func=AF.Exp))
        v_aexp = sA.n

        @bvec
        def _(vector):
            sX.wait(vector, v_const)
            sA.wait(vector, v_aexp)
            vector.tensor_scalar_mul(a_s[:], a_s[:], -1.0)
            vector.tensor_scalar_add(dp1_s[:], dp1_s[:], 1.0)
            vector.memset(ones_all[:], 1.0)
            vector.memset(zeros_s[:], 0.0)
            vector.memset(state_s[:], 0.0)
            vector.memset(sstate_s[:], 0.0)
            vector.memset(ssum_s[:], 0.0)
            vector.memset(hin_s[:], 0.0)
            vector.memset(eps_s[:], EPS)
            vector.tensor_copy(out=wxtb_s[:], in_=wxts_s[:])
            vector.tensor_copy(out=dtwtb_s[:], in_=dtwts_s[:])
            sV.inc(vector.tensor_copy(out=wrb_s[:], in_=wrs_s[:]))
        v_setup = sV.n

        xscope = contextlib.ExitStack()
        Ex = xscope.enter_context
        xstage = Ex(nc.sbuf_tensor("xstage", [P, 2, DM], F32))
        xTb = Ex(nc.sbuf_tensor("xTb", [P, DG, TOK], BF16))

        evac_done = [0] * NTT
        for tt in range(NTT):
            @block.sync
            def _(sync, tt=tt):
                if tt >= 2:
                    sV.wait(sync, evac_done[tt - 2])
                sXD[tt % 2].inc(sync.dma_start(
                    out=xstage[:, tt % 2, :],
                    in_=x_h.ap()[tt * P:(tt + 1) * P, :]), 16)
            v_x = sXD[tt % 2].n

            for dg in range(DG):
                @block.tensor
                def _(tensor, tt=tt, dg=dg, v_x=v_x):
                    bk = 2 + (dg % 2)
                    bwait_free(tensor, bk)
                    if dg == 0:
                        sXD[tt % 2].wait(tensor, v_x)
                        if tt == 0:
                            sV.wait(tensor, v_setup)
                    mm = tensor.transpose(
                        pb[bk][:, 0:P], xstage[:, tt % 2, dg * P:(dg + 1) * P],
                        id_s[:])
                    bank_prod[bk] = (sT, sT.inc(mm))

                @bvec
                def _(vector, tt=tt, dg=dg):
                    bk = 2 + (dg % 2)
                    bwait_prod(vector, bk)
                    vector.tensor_copy(
                        out=xT[:, dg, tt * P:(tt + 1) * P], in_=pb[bk][:, 0:P])
                    cp = vector.tensor_copy(
                        out=xTb[:, dg, tt * P:(tt + 1) * P],
                        in_=pb[bk][:, 0:P])
                    bank_free[bk] = (sV, sV.inc(cp))
            evac_done[tt] = sV.n
        v_xready = sV.n

        for nt in range(TOK // N5):
            @block.tensor
            def _(tensor, nt=nt):
                if nt == 0:
                    sV.wait(tensor, v_xready)
                bk = 4 + (nt % 2)
                bwait_free(tensor, bk)
                for dg in range(DG):
                    mm = tensor.matmul(
                        pb[bk][0:96, 0:N5], wxtb_s[:, dg, :],
                        xTb[:, dg, nt * N5:(nt + 1) * N5],
                        start=(dg == 0), stop=(dg == DG - 1))
                bank_prod[bk] = (sT, sT.inc(mm))

            @bscal
            def _(scalar, nt=nt):
                bk = 4 + (nt % 2)
                bwait_prod(scalar, bk)
                cp = scalar.copy(out=proj_s[:, nt * N5:(nt + 1) * N5],
                                 in_=pb[bk][0:96, 0:N5])
                bank_free[bk] = (sA, sA.inc(cp))
        v_proj = sA.n

        @bvec
        def _(vector):
            sA.wait(vector, v_proj)
            sV.inc(vector.tensor_copy(out=projb_s[:], in_=proj_s[0:R, :]))
        v_projb = sV.n
        keep or xscope.close()

        SH = S // 2
        sscope = contextlib.ExitStack()
        Es = sscope.enter_context
        dtall_s = Es(nc.sbuf_tensor("dtall_s", [P, DG, TC], F32))
        aux_s = Es(nc.sbuf_tensor("aux_s", [P, DG, TC], F32))
        bbc_s = Es(nc.sbuf_tensor("bbc_s", [P, TC, SH], F32))
        cbc_s = Es(nc.sbuf_tensor("cbc_s", [P, TC, SH], F32))
        da_s = Es(nc.sbuf_tensor("da_s", [P, 2, TC, SH], F32))
        u_s = Es(nc.sbuf_tensor("u_s", [P, TC, SH], F32))
        hh_s = Es(nc.sbuf_tensor("hh_s", [P, TC, SH], F32))
        dt2_s = Es(nc.sbuf_tensor("dt2_s", [P, TC], F32))
        da_prev = {0: 0, 1: 0}
        y_s = Es(nc.sbuf_tensor("y_s", [P, TC], F32))
        brow_s = Es(nc.sbuf_tensor("brow_s", [1, SH * TC], F32))
        crow_s = Es(nc.sbuf_tensor("crow_s", [1, SH * TC], F32))

        rows_waw = {"b": 0, "c": 0}

        def emit_rows(tc, sh, want_b, v_reader_done):
            t0 = tc * TC
            kinds = (["b"] if want_b else []) + ["c"]

            @block.sync
            def _(sync, tc=tc, sh=sh, kinds=tuple(kinds),
                  w_b=rows_waw["b"], w_c=rows_waw["c"]):
                sA.wait(sync, v_proj)
                for kind in kinds:
                    w = w_b if kind == "b" else w_c
                    if w:
                        sT.wait(sync, w)
                    base = 64 + sh * SH if kind == "b" else 80 + sh * SH
                    dst = brow_s if kind == "b" else crow_s
                    sR.inc(sync.dma_start(
                        out=dst[0:1, :],
                        in_=proj_s[base:base + SH, t0:t0 + TC]), 16)
            v_rows = sR.n

            for i, kind in enumerate(kinds):
                for si in range(SH):
                    bk = (i * SH + si) % 2
                    srcrow = brow_s if kind == "b" else crow_s

                    @block.tensor
                    def _(tensor, bk=bk, si=si, srcrow=srcrow, kind=kind,
                          first=(i == 0 and si == 0), v_rows=v_rows):
                        bwait_free(tensor, bk)
                        if first:
                            sR.wait(tensor, v_rows)
                        mm = tensor.matmul(
                            pb[bk][:, 0:TC], ones_all[0:1, :],
                            srcrow[0:1, si * TC:(si + 1) * TC],
                            start=True, stop=True)
                        bank_prod[bk] = (sT, sT.inc(mm))
                        rows_waw[kind] = sT.n

                    @bvec
                    def _(vector, bk=bk, si=si, kind=kind):
                        bwait_prod(vector, bk)
                        dst = bbc_s if kind == "b" else cbc_s
                        vector.tensor_copy(out=dst[:, :, si], in_=pb[bk][:, 0:TC])
                        bank_free[bk] = (sV, sV.n)
            return sV.n

        def emit_dt(tc, dg, v_waw_vec, phase, b, first):
            t0 = tc * TC

            @block.tensor
            def _(tensor):
                bwait_free(tensor, 2)
                if tc == 0 and dg == 0:
                    sV.wait(tensor, v_projb)
                mm = tensor.matmul(
                    pb[2][:, 0:TC], dtwtb_s[:, dg * P:(dg + 1) * P],
                    projb_s[:, t0:t0 + TC], start=True, stop=True)
                bank_prod[2] = (sT, sT.inc(mm))

            @bscal
            def _(scalar):
                bwait_prod(scalar, 2)
                if v_waw_vec:
                    sV.wait(scalar, v_waw_vec)
                bias = dtb_s[:, dg:dg + 1]
                scalar.activation(out=dt2_s[:], in_=pb[2][:, 0:TC],
                                  func=AF.Abs, bias=bias)
                scalar.activation(out=dt2_s[:], in_=dt2_s[:],
                                  func=AF.Exp, scale=-1.0)
                scalar.activation(out=dt2_s[:], in_=dt2_s[:],
                                  func=AF.Ln, bias=1.0)
                rl = scalar.activation(out=dtall_s[:, dg, :], in_=pb[2][:, 0:TC],
                                       func=AF.Relu, bias=bias)
                bank_free[2] = (sA, sA.inc(rl))
            v_sp = sA.n

            @bvec
            def _(vector, v_sp=v_sp, dg=dg, b=b, first=first, phase=phase,
                  v_waw_vec=v_waw_vec):
                if v_waw_vec:
                    vector.wait_ge(sV.h, v_waw_vec)
                sA.wait(vector, v_sp)
                dtv = dtall_s[:, dg, :]
                vector.tensor_add(dtv, dtv, dt2_s[:])
                if phase == "A":
                    vector.reduce_sum(out=red_s[:, 0:1], in_=dtv, axis=AX.X)
                    vector.tensor_add(ssum_s[:, b, dg:dg + 1],
                                      ssum_s[:, b, dg:dg + 1], red_s[:, 0:1])
                    f = vector.tensor_mul(aux_s[:, dg, :], dtv,
                                          xT[:, dg, t0:t0 + TC])
                else:
                    init = 0.0 if first else sstate_s[:, dg:dg + 1]
                    vector.tensor_tensor_scan(
                        out=aux_s[:, dg, :], data0=dtv, data1=zeros_s[:],
                        initial=init, op0=ALU.add, op1=ALU.add)
                    f = vector.tensor_copy(out=sstate_s[:, dg:dg + 1],
                                           in_=aux_s[:, dg, TC - 1:TC])
                sV.inc(f)
            return sV.n

        v_vec_done = 0
        for tc in range(NTC):
            b = tc // TCB
            first = (tc % TCB == 0)
            last = (tc % TCB == TCB - 1)
            t0 = tc * TC
            for dg in range(DG):
                emit_dt(tc, dg, v_vec_done, "A", b, first)
                v_vec_done = sV.n
            v_dtall = sV.n

            for sh in range(2):
                v_bc = emit_rows(tc, sh, True, v_vec_done)
                for dg in range(DG):
                    par = dg % 2

                    @bscal
                    def _(scalar, dg=dg, sh=sh, par=par,
                          v_w=max(v_dtall, da_prev[par])):
                        sV.wait(scalar, v_w)
                        for si in range(SH):
                            scalar.activation(
                                out=da_s[:, par, :, si], in_=dtall_s[:, dg, :],
                                func=AF.Exp,
                                scale=a_s[:, dg, sh * SH + si:sh * SH + si + 1])
                    v_da = sA.n

                    @bvec
                    def _(vector, dg=dg, sh=sh, t0=t0, first=first, par=par,
                          v_da=v_da):
                        dtxb = aux_s[:, dg, :].unsqueeze(2).broadcast_to(
                            [P, TC, SH])
                        vector.tensor_tensor(out=u_s[:], in0=bbc_s[:],
                                             in1=dtxb, op=ALU.mult)
                        sA.wait(vector, v_da)
                        for si in range(SH):
                            sg = sh * SH + si
                            init = 0.0 if first else state_s[:, dg, sg:sg + 1]
                            vector.tensor_tensor_scan(
                                out=hh_s[:, :, si], data0=da_s[:, par, :, si],
                                data1=u_s[:, :, si], initial=init,
                                op0=ALU.mult, op1=ALU.add)
                        vector.tensor_copy(
                            out=state_s[:, dg, sh * SH:(sh + 1) * SH],
                            in_=hh_s[:, TC - 1, :])
                        vector.tensor_mul(hh_s[:], hh_s[:], cbc_s[:])
                        vector.reduce_sum(out=y_s[:], in_=hh_s[:], axis=AX.X)
                        if sh == 0:
                            f = vector.scalar_tensor_tensor(
                                out=xT[:, dg, t0:t0 + TC],
                                in0=xT[:, dg, t0:t0 + TC],
                                scalar=dp1_s[:, dg:dg + 1], in1=y_s[:],
                                op0=ALU.mult, op1=ALU.add)
                        else:
                            f = vector.tensor_add(
                                xT[:, dg, t0:t0 + TC],
                                xT[:, dg, t0:t0 + TC], y_s[:])
                        sV.inc(f)
                    v_vec_done = sV.n
                    da_prev[par] = sV.n

            if last:
                @bvec
                def _(vector, b=b):
                    sV.inc(vector.tensor_copy(
                        out=hT_s[:, b, :],
                        in_=state_s[:].rearrange("p g s -> p (g s)")))
                v_vec_done = sV.n

        @bvec
        def _(vector, v_sw=sV.n):
            vector.wait_ge(sV.h, v_sw)
            vector.tensor_copy(out=pack_s[:, 0:P], in_=hT_s[:, 0, :])
            vector.tensor_copy(out=pack_s[:, P:2 * P], in_=hT_s[:, 1, :])
            sV.inc(vector.tensor_copy(
                out=pack_s[:, 2 * P:2 * P + 2 * DG],
                in_=ssum_s[:].rearrange("p b g -> p (b g)")))
        v_pack = sV.n

        @bgps
        def _(gpsimd):
            sV.wait(gpsimd, v_pack)
            sC.inc(gpsimd.dma_start(
                out=cci_h.ap().rearrange("(p c) -> p c", p=P),
                in_=pack_s[:]), 16)
            sC.wait(gpsimd)
            sCC.inc(gpsimd.collective_compute(
                "AllGather", ALU.bypass, replica_groups=[list(range(NCORE))],
                ins=[cci_h.ap()], outs=[cco_h.ap()]), 1)
            sCC.wait(gpsimd)
            sC.inc(gpsimd.dma_start(
                out=gath_s[:],
                in_=cco_h.ap().rearrange("(r p c) -> p r c", p=P, r=NCORE)), 16)
        v_gath = sC.n

        for b in range(NB):
            for j in range(NCORE - 1):
                @bvec
                def _(vector, b=b, j=j):
                    if b == 0 and j == 0:
                        sC.wait(vector, v_gath)
                    vector.tensor_scalar_mul(
                        tdg_s[:],
                        gath_s[:, j, 2 * P + b * DG:2 * P + (b + 1) * DG],
                        mask_s[:, j:j + 1])
                    smb = tdg_s[:].unsqueeze(2).broadcast_to([P, DG, S])
                    sV.inc(vector.tensor_tensor(out=edg_s[:], in0=a_s[:],
                                                in1=smb, op=ALU.mult))
                v_1 = sV.n

                @bscal
                def _(scalar, v_1=v_1):
                    sV.wait(scalar, v_1)
                    sA.inc(scalar.activation(out=edg_s[:], in_=edg_s[:],
                                             func=AF.Exp))
                v_2 = sA.n

                @bvec
                def _(vector, b=b, j=j, v_2=v_2, v_sw=sV.n):
                    vector.wait_ge(sV.h, v_sw)
                    sA.wait(vector, v_2)
                    hv = hin_s[:, b, :, :].rearrange("p g s -> p (g s)")
                    ev = edg_s[:].rearrange("p g s -> p (g s)")
                    vector.tensor_mul(hv, hv, ev)
                    vector.tensor_scalar_mul(
                        thp_s[:], gath_s[:, j, b * P:(b + 1) * P],
                        mask_s[:, j:j + 1])
                    sV.inc(vector.tensor_add(hv, hv, thp_s[:]))
        v_vec_done = sV.n

        for tc in range(NTC):
            b = tc // TCB
            first = (tc % TCB == 0)
            t0 = tc * TC
            for dg in range(DG):
                emit_dt(tc, dg, v_vec_done, "B", b, first)
                v_vec_done = sV.n
            v_sall = sV.n

            for sh in range(2):
                v_bc = emit_rows(tc, sh, False, v_vec_done)
                for dg in range(DG):
                    par = dg % 2

                    @bvec
                    def _(vector, dg=dg, sh=sh, par=par):
                        vector.tensor_tensor(
                            out=da_s[:, par],
                            in0=aux_s[:, dg, :].unsqueeze(2).broadcast_to(
                                [P, TC, SH]),
                            in1=a_s[:, dg, sh * SH:(sh + 1) * SH].unsqueeze(
                                1).broadcast_to([P, TC, SH]),
                            op=ALU.mult)
                    v_arg = sV.n

                    @bscal
                    def _(scalar, par=par, v_arg=v_arg):
                        sV.wait(scalar, v_arg)
                        scalar.activation(out=da_s[:, par], in_=da_s[:, par],
                                          func=AF.Exp)
                    v_e = sA.n

                    @bvec
                    def _(vector, b=b, dg=dg, sh=sh, t0=t0, par=par,
                          v_e=v_e):
                        sA.wait(vector, v_e)
                        for si in range(SH):
                            sg = sh * SH + si
                            vector.scalar_tensor_tensor(
                                out=u_s[:, :, si], in0=da_s[:, par, :, si],
                                scalar=hin_s[:, b, dg, sg:sg + 1],
                                in1=cbc_s[:, :, si], op0=ALU.mult, op1=ALU.mult)
                        vector.reduce_sum(out=y_s[:], in_=u_s[:], axis=AX.X)
                        sV.inc(vector.tensor_add(
                            xT[:, dg, t0:t0 + TC], xT[:, dg, t0:t0 + TC],
                            y_s[:]))
                    v_vec_done = sV.n
        v_phaseB_v = sV.n

        keep or sscope.close()
        keep or mscope.close()

        ln1b_s = E(nc.sbuf_tensor("ln1b_s", [P, DG, TOK], BF16, side="right"))
        acc_s = E(nc.sbuf_tensor("acc_s", [P, NTT, DM], F32, side="right"))

        lscope = contextlib.ExitStack()
        El = lscope.enter_context
        mrow_s = El(nc.sbuf_tensor("mrow_s", [1, TC], F32))
        rrow_s = El(nc.sbuf_tensor("rrow_s", [1, TC], F32))
        trow_s = El(nc.sbuf_tensor("trow_s", [1, TC], F32))
        sq_s = El(nc.sbuf_tensor("sq_s", [P, 2, TC], F32))

        v_rstdt = [0] * NTC
        sq_mm = [0, 0]
        v_bcast_prev = 0
        v_var_prev = 0
        for tc in range(NTC):
            t0 = tc * TC
            for dg in range(DG):
                @bvec
                def _(vector, dg=dg, t0=t0):
                    if dg >= 2:
                        sT.wait(vector, sq_mm[dg % 2])
                    vector.tensor_mul(sq_s[:, dg % 2, :],
                                      xT[:, dg, t0:t0 + TC],
                                      xT[:, dg, t0:t0 + TC])
                v_sq = sV.n

                @block.tensor
                def _(tensor, dg=dg, t0=t0, v_sq=v_sq):
                    sV.wait(tensor, v_sq)
                    if dg == 0:
                        bwait_free(tensor, 6)
                        bwait_free(tensor, 7)
                    tensor.matmul(pb[6][0:1, 0:TC], ones_all[:, 0:1],
                                  xT[:, dg, t0:t0 + TC],
                                  start=(dg == 0), stop=(dg == DG - 1))
                    mm = tensor.matmul(pb[7][0:1, 0:TC], ones_all[:, 0:1],
                                       sq_s[:, dg % 2, :],
                                       start=(dg == 0), stop=(dg == DG - 1))
                    sq_mm[dg % 2] = sT.inc(mm)
            v_stats = sT.n
            bank_prod[6] = (sT, v_stats)
            bank_prod[7] = (sT, v_stats)

            @bvec
            def _(vector, tc=tc, v_stats=v_stats, vb=v_bcast_prev):
                sT.wait(vector, v_stats)
                if tc > 0:
                    sT.wait(vector, vb)
                    sRS.wait(vector, v_rstdt[tc - 1])
                vector.tensor_scalar_mul(mrow_s[:], pb[6][0:1, 0:TC], 1.0 / DM)
                vector.tensor_scalar_mul(trow_s[:], pb[7][0:1, 0:TC], 1.0 / DM)
                vector.tensor_mul(rrow_s[:], mrow_s[:], mrow_s[:])
                vector.tensor_sub(trow_s[:], trow_s[:], rrow_s[:])
            v_var = sV.n
            bank_free[6] = (sV, v_var)
            bank_free[7] = (sV, v_var)

            @bscal
            def _(scalar, v_var=v_var):
                sV.wait(scalar, v_var)
                scalar.activation(out=trow_s[:], in_=trow_s[:],
                                  func=AF.Ln, bias=eps_s[0:1, 0:1])
                sA.inc(scalar.activation(out=rrow_s[:], in_=trow_s[:],
                                         func=AF.Exp, scale=-0.5))
            v_rstd = sA.n

            @block.tensor
            def _(tensor, tc=tc, v_rstd=v_rstd):
                sA.wait(tensor, v_rstd)
                bwait_free(tensor, 4)
                bwait_free(tensor, 5)
                tensor.matmul(pb[4][:, 0:TC], ones_all[0:1, :], mrow_s[:],
                              start=True, stop=True)
                mm = tensor.matmul(pb[5][:, 0:TC], ones_all[0:1, :], rrow_s[:],
                                   start=True, stop=True)
                sT.inc(mm)
            v_bcast = sT.n
            v_bcast_prev = v_bcast
            bank_prod[4] = (sT, v_bcast)
            bank_prod[5] = (sT, v_bcast)

            for dg in range(DG):
                @bvec
                def _(vector, dg=dg, t0=t0, v_bcast=v_bcast):
                    sT.wait(vector, v_bcast)
                    vector.tensor_sub(xT[:, dg, t0:t0 + TC],
                                      xT[:, dg, t0:t0 + TC], pb[4][:, 0:TC])
                    sV.inc(vector.tensor_tensor(
                        out=ln1b_s[:, dg, t0:t0 + TC],
                        in0=xT[:, dg, t0:t0 + TC], in1=pb[5][:, 0:TC],
                        op=ALU.mult))
            v_apply = sV.n
            bank_free[4] = (sV, v_apply)
            bank_free[5] = (sV, v_apply)

            @block.sync
            def _(sync, tc=tc, v_rstd=v_rstd):
                sA.wait(sync, v_rstd)
                sRS.inc(sync.dma_start(
                    out=rstd_d.ap()[tc * TC:(tc + 1) * TC].unsqueeze(0),
                    in_=rrow_s[0:1, :]), 16)
            v_rstdt[tc] = sRS.n
        v_ln1_v = sV.n

        @block.sync
        def _(sync):
            sRS.wait(sync, v_rstdt[NTC - 1])
            sRS.inc(sync.dma_start(
                out=rstdt_s[:],
                in_=rstd_d.ap().rearrange("(t p) -> p t", p=P),
                allow_slow_non_contiguous=True), 16)
        v_rstdt_load = sRS.n

        for tt in range(NTT):
            tc = (tt * P) // TC
            for dg in range(DG):
                @block.tensor
                def _(tensor, tt=tt, dg=dg):
                    bk = 2 + (dg % 2)
                    bwait_free(tensor, bk)
                    if tt == 0 and dg == 0:
                        sV.wait(tensor, v_ln1_v)
                    mm = tensor.transpose(pb[bk][:, 0:P],
                                          xT[:, dg, tt * P:(tt + 1) * P],
                                          id_s[:])
                    bank_prod[bk] = (sT, sT.inc(mm))

                @bvec
                def _(vector, tt=tt, dg=dg):
                    bk = 2 + (dg % 2)
                    bwait_prod(vector, bk)
                    if tt == 0 and dg == 0:
                        sRS.wait(vector, v_rstdt_load)
                    vector.tensor_single_scalar(
                        out=acc_s[:, tt, dg * P:(dg + 1) * P],
                        in_=pb[bk][:, 0:P], scalar=rstdt_s[:, tt:tt + 1],
                        op=ALU.mult)
                    bank_free[bk] = (sV, sV.n)

            @block.tensor
            def _(tensor, tt=tt):
                bwait_free(tensor, 6)
                for dg in range(DG):
                    tensor.matmul(pb[6][:, 0:NE], ln1b_s[:, dg, tt * P:(tt + 1) * P],
                                  wrb_s[:, dg, :], start=(dg == 0), stop=False)
                mm = tensor.matmul(pb[6][:, 0:NE], ones_all[0:1, :], br_s[:],
                                   start=False, stop=True)
                bank_prod[6] = (sT, sT.inc(mm))

            @bvec
            def _(vector):
                bwait_prod(vector, 6)
                sV.inc(vector.reduce_max(out=red_s[:, 0:1], in_=pb[6][:, 0:NE],
                                         axis=AX.X, negate=True))
            v_nm = sV.n

            @bscal
            def _(scalar, v_nm=v_nm):
                sV.wait(scalar, v_nm)
                e = scalar.activation(out=rscr_s[:, 0:NE], in_=pb[6][:, 0:NE],
                                      func=AF.Exp, bias=red_s[:, 0:1])
                bank_free[6] = (sA, sA.inc(e))
            v_e = sA.n

            @bvec
            def _(vector, tt=tt, v_e=v_e):
                sA.wait(vector, v_e)
                e = rscr_s[:, 0:NE]
                f = rscr_s[:, 8:8 + NE]
                w2 = rscr_s[:, 16:16 + NE]
                vector.reduce_max(out=red_s[:, 1:2], in_=e, axis=AX.X)
                vector.tensor_single_scalar(out=f, in_=e,
                                            scalar=red_s[:, 1:2],
                                            op=ALU.is_equal)
                vector.tensor_scalar(out=f, in0=f, scalar1=-1.0, scalar2=1.0,
                                     op0=ALU.mult, op1=ALU.add)
                vector.tensor_mul(w2, e, f)
                vector.reduce_max(out=red_s[:, 2:3], in_=w2, axis=AX.X)
                vector.tensor_single_scalar(out=f, in_=e,
                                            scalar=red_s[:, 2:3],
                                            op=ALU.is_ge)
                vector.tensor_mul(e, e, f)
                vector.reduce_sum(out=red_s[:, 3:4], in_=e, axis=AX.X)
                vector.reciprocal(out=red_s[:, 4:5], in_=red_s[:, 3:4])
                sV.inc(vector.tensor_scalar_mul(rtok_s[:, tt, :], e,
                                                red_s[:, 4:5]))
            v_rt = sV.n

            @block.tensor
            def _(tensor, tt=tt, v_rt=v_rt):
                bwait_free(tensor, 7)
                sV.wait(tensor, v_rt)
                mm = tensor.transpose(pb[7][0:NE, 0:P], rtok_s[:, tt, :],
                                      id_s[:])
                bank_prod[7] = (sT, sT.inc(mm))

            @bscal
            def _(scalar, tt=tt):
                bwait_prod(scalar, 7)
                cp = scalar.copy(out=rfm_s[:, tt * P:(tt + 1) * P],
                                 in_=pb[7][0:NE, 0:P])
                bank_free[7] = (sA, sA.inc(cp))
        v_accinit = sA.n
        v_rt_all = sV.n

        keep or lscope.close()
        keep or xT_scope.close()

        fscope = contextlib.ExitStack()
        Ef = fscope.enter_context
        stg = Ef(nc.sbuf_tensor("stg", [P, HQ], F32))
        w1qb = Ef(nc.sbuf_tensor("w1qb", [P, 2, DG, HQ], BF16))
        w2qb = Ef(nc.sbuf_tensor("w2qb", [P, 2, HC, DM], BF16))
        h_sb = Ef(nc.sbuf_tensor("h_sb", [P, HC, N5], BF16))
        out_s = Ef(nc.sbuf_tensor("out_s", [P, 2, DM], F32))

        NQT = NE * NQ
        W1B = (0, 1, 4, 5)
        EOB = (2, 3, 6, 7)
        v_w1_mmdone = {}
        v_w2_mmdone = {}
        v_castdone = {}
        v_b1 = {}
        v_b1_use = {}
        heread = {0: 0, 1: 0}
        cast_prev = [0]

        def emit_stream(QI):
            e, q = divmod(QI, NQ)
            slot = QI % 2
            if q == 0:
                @block.sync
                def _(sync, e=e, v=v_b1_use.get(e - 2, 0)):
                    if v:
                        sA.wait(sync, v)
                    sB1.inc(sync.dma_start(out=b1r_s[:, e % 2, :],
                                           in_=b1_h.ap()[e]), 16)
                v_b1[e] = sB1.n

            vfree = (v_w1_mmdone.get(QI - 2, 0), v_w2_mmdone.get(QI - 2, 0))
            for half in range(2):
                for ck in range(DG):
                    @block.sync
                    def _(sync, e=e, q=q, ck=ck, half=half, QI=QI,
                          vc=cast_prev[0]):
                        if QI == 0 and half == 0 and ck == 0:
                            sA.wait(sync, v_accinit)
                        if vc:
                            sG.wait(sync, vc)
                        if half == 0:
                            src_ap = w1_h.ap()[
                                e, ck * P:(ck + 1) * P,
                                q * HQ:(q + 1) * HQ]
                        else:
                            src_ap = w2_h.ap()[
                                e, q * HQ + ck * P:q * HQ + (ck + 1) * P, :]
                        sW1.inc(sync.dma_start(out=stg[:], in_=src_ap), 16)
                    v_dma = sW1.n

                    @bgps
                    def _(gpsimd, slot=slot, ck=ck, half=half, v_dma=v_dma,
                          vfree=vfree[half], first=(ck == 0)):
                        sW1.wait(gpsimd, v_dma)
                        if first and vfree:
                            sT.wait(gpsimd, vfree)
                        dst = (w1qb if half == 0 else w2qb)
                        gpsimd.tensor_copy(
                            out=dst[:, slot, ck, :], in_=stg[:])
                        cast_prev[0] = sG.n
            v_castdone[QI] = sG.n

        def emit_compute(QI):
            e, q = divmod(QI, NQ)
            slot = QI % 2
            b2eff = (e == 0 and q == 0)
            for t5 in range(NT5):
                t0 = t5 * N5
                gW = [0] * HC

                @block.tensor
                def _(tensor, e=e, q=q, QI=QI, t5=t5, slot=slot, t0=t0):
                    if t5 == 0:
                        sG.wait(tensor, v_castdone[QI])
                        if QI == 0:
                            sV.wait(tensor, v_ln1_v)
                    vrelu_base = sA.n
                    for hdc in range(HC):
                        bk = W1B[hdc % 4]
                        for dg in range(DG):
                            mm = tensor.matmul(
                                pb[bk][:, 0:N5],
                                w1qb[:, slot, dg, hdc * P:(hdc + 1) * P],
                                ln1b_s[:, dg, t0:t0 + N5],
                                start=(dg == 0), stop=(dg == DG - 1))
                            if dg == 0:
                                if hdc < 4:
                                    if bank_free[bk] is not None:
                                        c, v = bank_free[bk]
                                        mm._wait_ge(c.h, v)
                                else:
                                    mm._wait_ge(sA.h,
                                                vrelu_base + (hdc - 4) + 1)
                        gW[hdc] = sT.inc(mm)

                @bscal
                def _(scalar, e=e, q=q, vh=heread[0]):
                    for hdc in range(HC):
                        bk = W1B[hdc % 4]
                        if hdc == 0:
                            sB1.wait(scalar, v_b1[e])
                            if vh:
                                sT.wait(scalar, vh)
                        sT.wait(scalar, gW[hdc])
                        scalar.activation(
                            out=h_sb[:, hdc, :], in_=pb[bk][:, 0:N5],
                            func=AF.Relu,
                            bias=b1r_s[:, e % 2,
                                       q * HC + hdc:q * HC + hdc + 1])
                        bank_free[bk] = (sA, sA.n)
                v_hevac = sA.n
                v_w1_mmdone[QI] = sT.n
                mEO = [0] * (2 * T128)

                @block.tensor
                def _(tensor, e=e, QI=QI, t5=t5, slot=slot,
                      b2eff=b2eff, v_hevac=v_hevac):
                    sA.wait(tensor, v_hevac)
                    vstt_base = sV.n
                    for i in range(2 * T128):
                        t1, dh = divmod(i, 2)
                        ttg = t5 * T128 + t1
                        bk = EOB[i % 4]
                        for hdc in range(HC):
                            mm = tensor.matmul(
                                pb[bk][:, 0:512],
                                h_sb[:, hdc, t1 * P:(t1 + 1) * P],
                                w2qb[:, slot, hdc, dh * 512:(dh + 1) * 512],
                                start=(hdc == 0),
                                stop=(hdc == HC - 1 and not b2eff))
                            if hdc == 0:
                                if i < 4:
                                    if bank_free[bk] is not None:
                                        c, v = bank_free[bk]
                                        mm._wait_ge(c.h, v)
                                else:
                                    mm._wait_ge(sV.h,
                                                vstt_base + (i - 4) + 1)
                        if b2eff:
                            mm = tensor.matmul(
                                pb[bk][:, 0:512],
                                rfm_s[:, ttg * P:(ttg + 1) * P],
                                b2_s[:, dh * 512:(dh + 1) * 512],
                                start=False, stop=True)
                        mEO[i] = sT.inc(mm)
                heread[0] = sT.n
                v_w2_mmdone[QI] = sT.n

                @bvec
                def _(vector, e=e, QI=QI, t5=t5):
                    for i in range(2 * T128):
                        t1, dh = divmod(i, 2)
                        ttg = t5 * T128 + t1
                        bk = EOB[i % 4]
                        if i == 0 and QI == 0 and t5 == 0:
                            sA.wait(vector, v_accinit)
                            vector.wait_ge(sV.h, v_rt_all)
                        sT.wait(vector, mEO[i])
                        vector.scalar_tensor_tensor(
                            out=acc_s[:, ttg, dh * 512:(dh + 1) * 512],
                            in0=pb[bk][:, 0:512],
                            scalar=rtok_s[:, ttg, e:e + 1],
                            in1=acc_s[:, ttg, dh * 512:(dh + 1) * 512],
                            op0=ALU.mult, op1=ALU.add)
                        bank_free[bk] = (sV, sV.n)
            v_b1_use[e] = v_hevac

        emit_stream(0)
        for QI in range(NQT):
            if QI + 1 < NQT:
                emit_stream(QI + 1)
            emit_compute(QI)

        v_ffn_v = sV.n

        v_outdma = [0] * NTT
        for tt in range(NTT):
            @bvec
            def _(vector, tt=tt):
                if tt == 0:
                    sV.wait(vector, v_ffn_v)
                vector.reduce_sum(out=red_s[:, 0:1], in_=acc_s[:, tt, :],
                                  axis=AX.X)
                vector.tensor_scalar_mul(red_s[:, 1:2], red_s[:, 0:1],
                                         1.0 / DM)
                if tt >= 2:
                    sO[tt % 2].wait(vector, v_outdma[tt - 2])
                vector.scalar_tensor_tensor(
                    out=out_s[:, tt % 2, :], in0=acc_s[:, tt, :], scalar=1.0,
                    in1=acc_s[:, tt, :], op0=ALU.mult, op1=ALU.mult,
                    accum_out=red_s[:, 2:3])
                vector.tensor_scalar_mul(red_s[:, 3:4], red_s[:, 2:3],
                                         1.0 / DM)
                vector.tensor_mul(red_s[:, 4:5], red_s[:, 1:2], red_s[:, 1:2])
                sV.inc(vector.tensor_sub(red_s[:, 5:6], red_s[:, 3:4],
                                         red_s[:, 4:5]))
            v_var2 = sV.n

            @bscal
            def _(scalar, v_var2=v_var2):
                sV.wait(scalar, v_var2)
                scalar.activation(out=red_s[:, 6:7], in_=red_s[:, 5:6],
                                  func=AF.Ln, bias=eps_s[:, 0:1])
                sA.inc(scalar.activation(out=red_s[:, 6:7], in_=red_s[:, 6:7],
                                         func=AF.Exp, scale=-0.5))
            v_rstd2 = sA.n

            @bvec
            def _(vector, v_rstd2=v_rstd2):
                sA.wait(vector, v_rstd2)
                sV.inc(vector.scalar_tensor_tensor(
                    out=red_s[:, 7:8], in0=red_s[:, 1:2], scalar=-1.0,
                    in1=red_s[:, 6:7], op0=ALU.mult, op1=ALU.mult))
            v_nmr = sV.n

            @bvec
            def _(vector, tt=tt, v_nmr=v_nmr):
                vector.wait_ge(sV.h, v_nmr)
                nmb = red_s[:, 7:8].broadcast_to([P, DM])
                sV.inc(vector.scalar_tensor_tensor(
                    out=out_s[:, tt % 2, :], in0=acc_s[:, tt, :],
                    scalar=red_s[:, 6:7], in1=nmb,
                    op0=ALU.mult, op1=ALU.add))
            v_oev = sV.n

            @bgps
            def _(gpsimd, tt=tt, v_oev=v_oev):
                sV.wait(gpsimd, v_oev)
                sO[tt % 2].inc(gpsimd.dma_start(
                    out=out_h.ap()[tt * P:(tt + 1) * P, :],
                    in_=out_s[:, tt % 2, :]), 16)
            v_outdma[tt] = sO[tt % 2].n

        @bgps
        def _(gpsimd):
            sO[0].wait(gpsimd)
            sO[1].wait(gpsimd)

        keep or fscope.close()
        if keep:
            for sc in (fscope, lscope, sscope, xscope, mscope, xT_scope):
                sc.close()

    return nc


def prepare_in_maps(L0_attn_x, L0_attn_A_log, L0_attn_D, L0_attn_Wx,
                    L0_attn_dt_W, L0_attn_dt_b, L0_ffn_W1, L0_ffn_b1,
                    L0_ffn_W2, L0_ffn_b2, L0_ffn_Wr, L0_ffn_br):
    CL = L0_attn_x.shape[1] // NCORE
    x = np.ascontiguousarray(L0_attn_x, dtype=np.float32)
    c = np.ascontiguousarray

    def pd(v):
        v = np.asarray(v, dtype=np.float32)
        return c(v.reshape(DG, P, *v.shape[1:]).transpose(
            1, 0, *range(2, v.ndim + 1)))

    shared = {
        "WxT": c(np.asarray(L0_attn_Wx, np.float32).T),
        "dtWT": c(np.asarray(L0_attn_dt_W, np.float32).T),
        "A_pd": pd(L0_attn_A_log),
        "D_pd": pd(L0_attn_D),
        "dtb_pd": pd(L0_attn_dt_b),
        "ident": np.eye(P, dtype=np.float32),
        "Wr": c(np.asarray(L0_ffn_Wr, np.float32)),
        "br_r": c(np.asarray(L0_ffn_br, np.float32).reshape(1, NE)),
        "b2": c(np.asarray(L0_ffn_b2, np.float32)),
        "b1r": c(np.asarray(L0_ffn_b1, np.float32).reshape(
            NE, H // P, P).transpose(0, 2, 1)),
        "W1": c(np.asarray(L0_ffn_W1, np.float32)),
        "W2": c(np.asarray(L0_ffn_W2, np.float32)),
    }

    in_maps = []
    for cid in range(NCORE):
        m = dict(shared)
        m["x01"] = c(np.concatenate(
            [x[0, cid * CL:(cid + 1) * CL], x[1, cid * CL:(cid + 1) * CL]],
            axis=0))
        m["maskv"] = c(np.broadcast_to(
            (np.arange(NCORE) < cid).astype(np.float32), (P, NCORE)))
        in_maps.append(m)
    return in_maps, CL


def assemble_out(results, CL):
    out = np.empty((NB, NCORE * CL, DM), dtype=np.float32)
    for cid in range(NCORE):
        o = results[cid]["out"]
        out[0, cid * CL:(cid + 1) * CL] = o[:CL]
        out[1, cid * CL:(cid + 1) * CL] = o[CL:]
    return out


def kernel(L0_attn_x, L0_attn_A_log, L0_attn_D, L0_attn_Wx, L0_attn_dt_W,
           L0_attn_dt_b, L0_ffn_W1, L0_ffn_b1, L0_ffn_W2, L0_ffn_b2,
           L0_ffn_Wr, L0_ffn_br):
    in_maps, CL = prepare_in_maps(
        L0_attn_x, L0_attn_A_log, L0_attn_D, L0_attn_Wx, L0_attn_dt_W,
        L0_attn_dt_b, L0_ffn_W1, L0_ffn_b1, L0_ffn_W2, L0_ffn_b2,
        L0_ffn_Wr, L0_ffn_br)
    nc = build_nc(CL=CL)
    res = run_bass_kernel_spmd(nc, in_maps, core_ids=list(range(NCORE)))
    return assemble_out(res.results, CL)


# version 09
# speedup vs baseline: 1.1263x; 1.0917x over previous
import contextlib
import numpy as np
import concourse.bass as bass
import concourse.mybir as mybir
from concourse.bass_utils import run_bass_kernel_spmd

F32 = mybir.dt.float32
BF16 = mybir.dt.bfloat16
AF = mybir.ActivationFunctionType
ALU = mybir.AluOpType
AX = mybir.AxisListType

P = 128
DM = 1024
DG = 8
S = 16
R = 64
NB = 2
NCORE = 8
H = 4096
NE = 8
NQ = 4
HQ = H // NQ
HC = HQ // P
EPS = 1e-5


class Ctr:
    def __init__(self, ctx, nc, name):
        self.h = ctx.enter_context(nc.semaphore(name))
        self.n = 0
        self.seen = {}

    def inc(self, instr, k=1):
        if id(instr) in self.seen:
            return self.n
        instr.then_inc(self.h, k)
        self.n += k
        return self.n

    def wait(self, eng, at=None):
        eng.wait_ge(self.h, self.n if at is None else at)


class ChainEng:
    _PASS = frozenset(("wait_ge", "drain"))
    _NOAUTO = frozenset(("dma_start", "collective_compute"))

    def __init__(self, eng, ctr):
        self._e = eng
        self._c = ctr

    def __getattr__(self, name):
        f = getattr(self._e, name)
        if name in ChainEng._PASS or name in ChainEng._NOAUTO \
                or not callable(f):
            return f
        c = self._c

        def g(*a, **k):
            n0 = c.n
            ins = f(*a, **k)
            if n0:
                ins._wait_ge(c.h, n0)
            c.inc(ins)
            c.seen[id(ins)] = ins
            return ins
        return g


def chained_block(deco, ctr):
    def d(fn):
        def inner(eng):
            return fn(ChainEng(eng, ctr))
        return deco(inner)
    return d


def build_nc(CL=1024, TC=256, keep=False):
    TOK = NB * CL
    NTC = TOK // TC
    TCB = CL // TC
    NTT = TOK // P
    N5 = min(512, TOK)
    NT5 = TOK // N5
    T128 = N5 // P
    PACKC = 2 * P + 2 * DG

    nc = bass.Bass("TRN2", target_bir_lowering=False)

    x_h = nc.dram_tensor("x01", [TOK, DM], F32, kind="ExternalInput")
    wxt_h = nc.dram_tensor("WxT", [DM, 96], F32, kind="ExternalInput")
    dtwt_h = nc.dram_tensor("dtWT", [R, DM], F32, kind="ExternalInput")
    a_h = nc.dram_tensor("A_pd", [P, DG, S], F32, kind="ExternalInput")
    d_h = nc.dram_tensor("D_pd", [P, DG], F32, kind="ExternalInput")
    dtb_h = nc.dram_tensor("dtb_pd", [P, DG], F32, kind="ExternalInput")
    mask_h = nc.dram_tensor("maskv", [P, NCORE], F32, kind="ExternalInput")
    id_h = nc.dram_tensor("ident", [P, P], F32, kind="ExternalInput")
    wr_h = nc.dram_tensor("Wr", [DM, NE], F32, kind="ExternalInput")
    br_h = nc.dram_tensor("br_r", [1, NE], F32, kind="ExternalInput")
    b2_h = nc.dram_tensor("b2", [NE, DM], F32, kind="ExternalInput")
    b1_h = nc.dram_tensor("b1r", [NE, P, H // P], F32, kind="ExternalInput")
    w1_h = nc.dram_tensor("W1", [NE, DM, H], F32, kind="ExternalInput")
    w2_h = nc.dram_tensor("W2", [NE, H, DM], F32, kind="ExternalInput")
    out_h = nc.dram_tensor("out", [TOK, DM], F32, kind="ExternalOutput")
    cci_h = nc.dram_tensor("cc_in", [P * PACKC], F32, kind="Internal")
    rstd_d = nc.dram_tensor("rstd_d", [NB * CL], F32, kind="Internal")
    cco_h = nc.dram_tensor("cc_out", [NCORE * P * PACKC], F32, kind="Internal")

    ctx = contextlib.ExitStack()
    with ctx:
        E = ctx.enter_context

        sX = Ctr(ctx, nc, "sX")
        sXD = [Ctr(ctx, nc, "sXD0"), Ctr(ctx, nc, "sXD1")]
        sR = Ctr(ctx, nc, "sR")
        sB1 = Ctr(ctx, nc, "sB1")
        sRS = Ctr(ctx, nc, "sRS")
        sW1 = Ctr(ctx, nc, "sW1")
        sW2 = Ctr(ctx, nc, "sW2")
        sT = Ctr(ctx, nc, "sT")
        sA = Ctr(ctx, nc, "sA")
        sV = Ctr(ctx, nc, "sV")
        sG = Ctr(ctx, nc, "sG")
        sC = Ctr(ctx, nc, "sC")
        sCC = Ctr(ctx, nc, "sCC")
        sO = [Ctr(ctx, nc, "sO0"), Ctr(ctx, nc, "sO1")]

        id_s = E(nc.sbuf_tensor("id_s", [P, P], F32))
        ones_all = E(nc.sbuf_tensor("ones_all", [P, P], F32))
        a_s = E(nc.sbuf_tensor("a_s", [P, DG, S], F32))
        dp1_s = E(nc.sbuf_tensor("dp1_s", [P, DG], F32))
        dtb_s = E(nc.sbuf_tensor("dtb_s", [P, DG], F32))
        mask_s = E(nc.sbuf_tensor("mask_s", [P, NCORE], F32))
        wxtb_s = E(nc.sbuf_tensor("wxtb_s", [P, DG, 96], BF16))
        dtwtb_s = E(nc.sbuf_tensor("dtwtb_s", [R, DM], BF16))
        wrb_s = E(nc.sbuf_tensor("wrb_s", [P, DG, NE], BF16))
        br_s = E(nc.sbuf_tensor("br_s", [1, NE], F32))
        b2_s = E(nc.sbuf_tensor("b2_s", [NE, DM], F32))
        b1r_s = E(nc.sbuf_tensor("b1r_s", [P, 2, H // P], F32))
        zeros_s = E(nc.sbuf_tensor("zeros_s", [P, TC], F32))
        state_s = E(nc.sbuf_tensor("state_s", [P, DG, S], F32))
        sstate_s = E(nc.sbuf_tensor("sstate_s", [P, DG], F32))
        ssum_s = E(nc.sbuf_tensor("ssum_s", [P, NB, DG], F32))
        hT_s = E(nc.sbuf_tensor("hT_s", [P, NB, P], F32))
        hin_s = E(nc.sbuf_tensor("hin_s", [P, NB, DG, S], F32))
        red_s = E(nc.sbuf_tensor("red_s", [P, 8], F32))
        rscr_s = E(nc.sbuf_tensor("rscr_s", [P, 24], F32))
        rfm_s = E(nc.sbuf_tensor("rfm_s", [NE, TOK], F32))
        rtok_s = E(nc.sbuf_tensor("rtok_s", [P, NTT, NE], F32))
        rstdt_s = E(nc.sbuf_tensor("rstdt_s", [P, NTT], F32))
        edg_s = E(nc.sbuf_tensor("edg_s", [P, DG, S], F32))
        tdg_s = E(nc.sbuf_tensor("tdg_s", [P, DG], F32))
        thp_s = E(nc.sbuf_tensor("thp_s", [P, P], F32))
        eps_s = E(nc.sbuf_tensor("eps_s", [P, 1], F32))

        pb = [E(nc.psum_tensor(f"pb{i}", [P, 512], F32)) for i in range(8)]
        bank_free = [None] * 8
        bank_prod = [None] * 8

        def bwait_free(eng, bk):
            if bank_free[bk] is not None:
                c, v = bank_free[bk]
                c.wait(eng, v)

        def bwait_prod(eng, bk):
            c, v = bank_prod[bk]
            c.wait(eng, v)

        block = E(nc.Block())
        bvec = chained_block(block.vector, sV)
        bscal = chained_block(block.scalar, sA)
        bgps = chained_block(block.gpsimd, sG)

        xT_scope = contextlib.ExitStack()
        xT = xT_scope.enter_context(nc.sbuf_tensor("xT", [P, DG, TOK], F32))

        mscope = contextlib.ExitStack()
        Em = mscope.enter_context
        wxts_s = Em(nc.sbuf_tensor("wxts_s", [P, DG, 96], F32))
        dtwts_s = Em(nc.sbuf_tensor("dtwts_s", [R, DM], F32))
        wrs_s = Em(nc.sbuf_tensor("wrs_s", [P, DG, NE], F32))
        proj_s = Em(nc.sbuf_tensor("proj_s", [96, TOK], F32))
        projb_s = Em(nc.sbuf_tensor("projb_s", [R, TOK], BF16))
        gath_s = Em(nc.sbuf_tensor("gath_s", [P, NCORE, PACKC], F32))
        pack_s = Em(nc.sbuf_tensor("pack_s", [P, PACKC], F32))

        @block.sync
        def _(sync):
            for dst, src in ((id_s, id_h), (a_s, a_h), (dp1_s, d_h),
                             (dtb_s, dtb_h), (mask_s, mask_h), (br_s, br_h),
                             (b2_s, b2_h)):
                sX.inc(sync.dma_start(out=dst[:], in_=src.ap()), 16)
            sX.inc(sync.dma_start(
                out=wxts_s[:],
                in_=wxt_h.ap().rearrange("(g p) m -> p g m", p=P)), 16)
            sX.inc(sync.dma_start(out=dtwts_s[:], in_=dtwt_h.ap()), 16)
            sX.inc(sync.dma_start(
                out=wrs_s[:],
                in_=wr_h.ap().rearrange("(g p) e -> p g e", p=P)), 16)
        v_const = sX.n

        @bscal
        def _(scalar):
            sX.wait(scalar, v_const)
            sA.inc(scalar.activation(out=a_s[:], in_=a_s[:], func=AF.Exp))
        v_aexp = sA.n

        @bvec
        def _(vector):
            sX.wait(vector, v_const)
            sA.wait(vector, v_aexp)
            vector.tensor_scalar_mul(a_s[:], a_s[:], -1.0)
            vector.tensor_scalar_add(dp1_s[:], dp1_s[:], 1.0)
            vector.memset(ones_all[:], 1.0)
            vector.memset(zeros_s[:], 0.0)
            vector.memset(state_s[:], 0.0)
            vector.memset(sstate_s[:], 0.0)
            vector.memset(ssum_s[:], 0.0)
            vector.memset(hin_s[:], 0.0)
            vector.memset(eps_s[:], EPS)
            vector.tensor_copy(out=wxtb_s[:], in_=wxts_s[:])
            vector.tensor_copy(out=dtwtb_s[:], in_=dtwts_s[:])
            sV.inc(vector.tensor_copy(out=wrb_s[:], in_=wrs_s[:]))
        v_setup = sV.n

        xscope = contextlib.ExitStack()
        Ex = xscope.enter_context
        xstage = Ex(nc.sbuf_tensor("xstage", [P, 2, DM], F32))
        xTb = Ex(nc.sbuf_tensor("xTb", [P, DG, TOK], BF16))

        evac_done = [0] * NTT
        for tt in range(NTT):
            @block.sync
            def _(sync, tt=tt):
                if tt >= 2:
                    sV.wait(sync, evac_done[tt - 2])
                sXD[tt % 2].inc(sync.dma_start(
                    out=xstage[:, tt % 2, :],
                    in_=x_h.ap()[tt * P:(tt + 1) * P, :]), 16)
            v_x = sXD[tt % 2].n

            for dg in range(DG):
                @block.tensor
                def _(tensor, tt=tt, dg=dg, v_x=v_x):
                    bk = 2 + (dg % 2)
                    bwait_free(tensor, bk)
                    if dg == 0:
                        sXD[tt % 2].wait(tensor, v_x)
                        if tt == 0:
                            sV.wait(tensor, v_setup)
                    mm = tensor.transpose(
                        pb[bk][:, 0:P], xstage[:, tt % 2, dg * P:(dg + 1) * P],
                        id_s[:])
                    bank_prod[bk] = (sT, sT.inc(mm))

                @bvec
                def _(vector, tt=tt, dg=dg):
                    bk = 2 + (dg % 2)
                    bwait_prod(vector, bk)
                    vector.tensor_copy(
                        out=xT[:, dg, tt * P:(tt + 1) * P], in_=pb[bk][:, 0:P])
                    cp = vector.tensor_copy(
                        out=xTb[:, dg, tt * P:(tt + 1) * P],
                        in_=pb[bk][:, 0:P])
                    bank_free[bk] = (sV, sV.inc(cp))
            evac_done[tt] = sV.n
        v_xready = sV.n

        for nt in range(TOK // N5):
            @block.tensor
            def _(tensor, nt=nt):
                if nt == 0:
                    sV.wait(tensor, v_xready)
                bk = 4 + (nt % 2)
                bwait_free(tensor, bk)
                for dg in range(DG):
                    mm = tensor.matmul(
                        pb[bk][0:96, 0:N5], wxtb_s[:, dg, :],
                        xTb[:, dg, nt * N5:(nt + 1) * N5],
                        start=(dg == 0), stop=(dg == DG - 1))
                bank_prod[bk] = (sT, sT.inc(mm))

            @bscal
            def _(scalar, nt=nt):
                bk = 4 + (nt % 2)
                bwait_prod(scalar, bk)
                cp = scalar.copy(out=proj_s[:, nt * N5:(nt + 1) * N5],
                                 in_=pb[bk][0:96, 0:N5])
                bank_free[bk] = (sA, sA.inc(cp))
        v_proj = sA.n

        @bvec
        def _(vector):
            sA.wait(vector, v_proj)
            sV.inc(vector.tensor_copy(out=projb_s[:], in_=proj_s[0:R, :]))
        v_projb = sV.n
        keep or xscope.close()

        SH = S // 2
        sscope = contextlib.ExitStack()
        Es = sscope.enter_context
        dtall_s = Es(nc.sbuf_tensor("dtall_s", [P, DG, TC], F32))
        aux_s = Es(nc.sbuf_tensor("aux_s", [P, DG, TC], F32))
        bbc_s = Es(nc.sbuf_tensor("bbc_s", [P, TC, SH], F32))
        cbc_s = Es(nc.sbuf_tensor("cbc_s", [P, TC, SH], F32))
        da_s = Es(nc.sbuf_tensor("da_s", [P, 2, TC, SH], F32))
        u_s = Es(nc.sbuf_tensor("u_s", [P, TC, SH], F32))
        hh_s = Es(nc.sbuf_tensor("hh_s", [P, TC, SH], F32))
        dt2_s = Es(nc.sbuf_tensor("dt2_s", [P, TC], F32))
        da_prev = {0: 0, 1: 0}
        y_s = Es(nc.sbuf_tensor("y_s", [P, TC], F32))
        brow_s = Es(nc.sbuf_tensor("brow_s", [1, SH * TC], F32))
        crow_s = Es(nc.sbuf_tensor("crow_s", [1, SH * TC], F32))

        rows_waw = {"b": 0, "c": 0}

        def emit_rows(tc, sh, want_b, v_reader_done):
            t0 = tc * TC
            kinds = (["b"] if want_b else []) + ["c"]

            @block.sync
            def _(sync, tc=tc, sh=sh, kinds=tuple(kinds),
                  w_b=rows_waw["b"], w_c=rows_waw["c"]):
                sA.wait(sync, v_proj)
                for kind in kinds:
                    w = w_b if kind == "b" else w_c
                    if w:
                        sT.wait(sync, w)
                    base = 64 + sh * SH if kind == "b" else 80 + sh * SH
                    dst = brow_s if kind == "b" else crow_s
                    sR.inc(sync.dma_start(
                        out=dst[0:1, :],
                        in_=proj_s[base:base + SH, t0:t0 + TC]), 16)
            v_rows = sR.n

            for i, kind in enumerate(kinds):
                for si in range(SH):
                    bk = (i * SH + si) % 2
                    srcrow = brow_s if kind == "b" else crow_s

                    @block.tensor
                    def _(tensor, bk=bk, si=si, srcrow=srcrow, kind=kind,
                          first=(i == 0 and si == 0), v_rows=v_rows):
                        bwait_free(tensor, bk)
                        if first:
                            sR.wait(tensor, v_rows)
                        mm = tensor.matmul(
                            pb[bk][:, 0:TC], ones_all[0:1, :],
                            srcrow[0:1, si * TC:(si + 1) * TC],
                            start=True, stop=True)
                        bank_prod[bk] = (sT, sT.inc(mm))
                        rows_waw[kind] = sT.n

                    @bvec
                    def _(vector, bk=bk, si=si, kind=kind):
                        bwait_prod(vector, bk)
                        dst = bbc_s if kind == "b" else cbc_s
                        vector.tensor_copy(out=dst[:, :, si], in_=pb[bk][:, 0:TC])
                        bank_free[bk] = (sV, sV.n)
            return sV.n

        def emit_dt(tc, dg, v_waw_vec, phase, b, first):
            t0 = tc * TC

            @block.tensor
            def _(tensor):
                bwait_free(tensor, 2)
                if tc == 0 and dg == 0:
                    sV.wait(tensor, v_projb)
                mm = tensor.matmul(
                    pb[2][:, 0:TC], dtwtb_s[:, dg * P:(dg + 1) * P],
                    projb_s[:, t0:t0 + TC], start=True, stop=True)
                bank_prod[2] = (sT, sT.inc(mm))

            @bscal
            def _(scalar):
                bwait_prod(scalar, 2)
                if v_waw_vec:
                    sV.wait(scalar, v_waw_vec)
                bias = dtb_s[:, dg:dg + 1]
                scalar.activation(out=dt2_s[:], in_=pb[2][:, 0:TC],
                                  func=AF.Abs, bias=bias)
                scalar.activation(out=dt2_s[:], in_=dt2_s[:],
                                  func=AF.Exp, scale=-1.0)
                scalar.activation(out=dt2_s[:], in_=dt2_s[:],
                                  func=AF.Ln, bias=1.0)
                rl = scalar.activation(out=dtall_s[:, dg, :], in_=pb[2][:, 0:TC],
                                       func=AF.Relu, bias=bias)
                bank_free[2] = (sA, sA.inc(rl))
            v_sp = sA.n

            @bvec
            def _(vector, v_sp=v_sp, dg=dg, b=b, first=first, phase=phase,
                  v_waw_vec=v_waw_vec):
                if v_waw_vec:
                    vector.wait_ge(sV.h, v_waw_vec)
                sA.wait(vector, v_sp)
                dtv = dtall_s[:, dg, :]
                vector.tensor_add(dtv, dtv, dt2_s[:])
                if phase == "A":
                    vector.reduce_sum(out=red_s[:, 0:1], in_=dtv, axis=AX.X)
                    vector.tensor_add(ssum_s[:, b, dg:dg + 1],
                                      ssum_s[:, b, dg:dg + 1], red_s[:, 0:1])
                    f = vector.tensor_mul(aux_s[:, dg, :], dtv,
                                          xT[:, dg, t0:t0 + TC])
                else:
                    init = 0.0 if first else sstate_s[:, dg:dg + 1]
                    vector.tensor_tensor_scan(
                        out=aux_s[:, dg, :], data0=dtv, data1=zeros_s[:],
                        initial=init, op0=ALU.add, op1=ALU.add)
                    f = vector.tensor_copy(out=sstate_s[:, dg:dg + 1],
                                           in_=aux_s[:, dg, TC - 1:TC])
                sV.inc(f)
            return sV.n

        v_vec_done = 0
        for tc in range(NTC):
            b = tc // TCB
            first = (tc % TCB == 0)
            last = (tc % TCB == TCB - 1)
            t0 = tc * TC
            for dg in range(DG):
                emit_dt(tc, dg, v_vec_done, "A", b, first)
                v_vec_done = sV.n
            v_dtall = sV.n

            for sh in range(2):
                v_bc = emit_rows(tc, sh, True, v_vec_done)
                for dg in range(DG):
                    par = dg % 2

                    @bscal
                    def _(scalar, dg=dg, sh=sh, par=par,
                          v_w=max(v_dtall, da_prev[par])):
                        sV.wait(scalar, v_w)
                        for si in range(SH):
                            scalar.activation(
                                out=da_s[:, par, :, si], in_=dtall_s[:, dg, :],
                                func=AF.Exp,
                                scale=a_s[:, dg, sh * SH + si:sh * SH + si + 1])
                    v_da = sA.n

                    @bvec
                    def _(vector, dg=dg, sh=sh, t0=t0, first=first, par=par,
                          v_da=v_da):
                        dtxb = aux_s[:, dg, :].unsqueeze(2).broadcast_to(
                            [P, TC, SH])
                        vector.tensor_tensor(out=u_s[:], in0=bbc_s[:],
                                             in1=dtxb, op=ALU.mult)
                        sA.wait(vector, v_da)
                        for si in range(SH):
                            sg = sh * SH + si
                            init = 0.0 if first else state_s[:, dg, sg:sg + 1]
                            vector.tensor_tensor_scan(
                                out=hh_s[:, :, si], data0=da_s[:, par, :, si],
                                data1=u_s[:, :, si], initial=init,
                                op0=ALU.mult, op1=ALU.add)
                        vector.tensor_copy(
                            out=state_s[:, dg, sh * SH:(sh + 1) * SH],
                            in_=hh_s[:, TC - 1, :])
                        vector.tensor_mul(hh_s[:], hh_s[:], cbc_s[:])
                        vector.reduce_sum(out=y_s[:], in_=hh_s[:], axis=AX.X)
                        if sh == 0:
                            f = vector.scalar_tensor_tensor(
                                out=xT[:, dg, t0:t0 + TC],
                                in0=xT[:, dg, t0:t0 + TC],
                                scalar=dp1_s[:, dg:dg + 1], in1=y_s[:],
                                op0=ALU.mult, op1=ALU.add)
                        else:
                            f = vector.tensor_add(
                                xT[:, dg, t0:t0 + TC],
                                xT[:, dg, t0:t0 + TC], y_s[:])
                        sV.inc(f)
                    v_vec_done = sV.n
                    da_prev[par] = sV.n

            if last:
                @bvec
                def _(vector, b=b):
                    sV.inc(vector.tensor_copy(
                        out=hT_s[:, b, :],
                        in_=state_s[:].rearrange("p g s -> p (g s)")))
                v_vec_done = sV.n

        @bvec
        def _(vector, v_sw=sV.n):
            vector.wait_ge(sV.h, v_sw)
            vector.tensor_copy(out=pack_s[:, 0:P], in_=hT_s[:, 0, :])
            vector.tensor_copy(out=pack_s[:, P:2 * P], in_=hT_s[:, 1, :])
            sV.inc(vector.tensor_copy(
                out=pack_s[:, 2 * P:2 * P + 2 * DG],
                in_=ssum_s[:].rearrange("p b g -> p (b g)")))
        v_pack = sV.n

        @bgps
        def _(gpsimd):
            sV.wait(gpsimd, v_pack)
            sC.inc(gpsimd.dma_start(
                out=cci_h.ap().rearrange("(p c) -> p c", p=P),
                in_=pack_s[:]), 16)
            sC.wait(gpsimd)
            sCC.inc(gpsimd.collective_compute(
                "AllGather", ALU.bypass, replica_groups=[list(range(NCORE))],
                ins=[cci_h.ap()], outs=[cco_h.ap()]), 1)
            sCC.wait(gpsimd)
            sC.inc(gpsimd.dma_start(
                out=gath_s[:],
                in_=cco_h.ap().rearrange("(r p c) -> p r c", p=P, r=NCORE)), 16)
        v_gath = sC.n

        for b in range(NB):
            for j in range(NCORE - 1):
                @bvec
                def _(vector, b=b, j=j):
                    if b == 0 and j == 0:
                        sC.wait(vector, v_gath)
                    vector.tensor_scalar_mul(
                        tdg_s[:],
                        gath_s[:, j, 2 * P + b * DG:2 * P + (b + 1) * DG],
                        mask_s[:, j:j + 1])
                    smb = tdg_s[:].unsqueeze(2).broadcast_to([P, DG, S])
                    sV.inc(vector.tensor_tensor(out=edg_s[:], in0=a_s[:],
                                                in1=smb, op=ALU.mult))
                v_1 = sV.n

                @bscal
                def _(scalar, v_1=v_1):
                    sV.wait(scalar, v_1)
                    sA.inc(scalar.activation(out=edg_s[:], in_=edg_s[:],
                                             func=AF.Exp))
                v_2 = sA.n

                @bvec
                def _(vector, b=b, j=j, v_2=v_2, v_sw=sV.n):
                    vector.wait_ge(sV.h, v_sw)
                    sA.wait(vector, v_2)
                    hv = hin_s[:, b, :, :].rearrange("p g s -> p (g s)")
                    ev = edg_s[:].rearrange("p g s -> p (g s)")
                    vector.tensor_mul(hv, hv, ev)
                    vector.tensor_scalar_mul(
                        thp_s[:], gath_s[:, j, b * P:(b + 1) * P],
                        mask_s[:, j:j + 1])
                    sV.inc(vector.tensor_add(hv, hv, thp_s[:]))
        v_vec_done = sV.n

        for tc in range(NTC):
            b = tc // TCB
            first = (tc % TCB == 0)
            t0 = tc * TC
            for dg in range(DG):
                emit_dt(tc, dg, v_vec_done, "B", b, first)
                v_vec_done = sV.n
            v_sall = sV.n

            for sh in range(2):
                v_bc = emit_rows(tc, sh, False, v_vec_done)
                for dg in range(DG):
                    par = dg % 2

                    @bscal
                    def _(scalar, dg=dg, sh=sh, par=par,
                          v_w=max(v_sall, da_prev[par])):
                        sV.wait(scalar, v_w)
                        for si in range(SH):
                            scalar.activation(
                                out=da_s[:, par, :, si], in_=aux_s[:, dg, :],
                                func=AF.Exp,
                                scale=a_s[:, dg, sh * SH + si:sh * SH + si + 1])
                    v_e = sA.n

                    @bvec
                    def _(vector, b=b, dg=dg, sh=sh, t0=t0, par=par,
                          v_e=v_e):
                        sA.wait(vector, v_e)
                        for si in range(SH):
                            sg = sh * SH + si
                            vector.scalar_tensor_tensor(
                                out=u_s[:, :, si], in0=da_s[:, par, :, si],
                                scalar=hin_s[:, b, dg, sg:sg + 1],
                                in1=cbc_s[:, :, si], op0=ALU.mult, op1=ALU.mult)
                        vector.reduce_sum(out=y_s[:], in_=u_s[:], axis=AX.X)
                        sV.inc(vector.tensor_add(
                            xT[:, dg, t0:t0 + TC], xT[:, dg, t0:t0 + TC],
                            y_s[:]))
                    v_vec_done = sV.n
                    da_prev[par] = sV.n
        v_phaseB_v = sV.n

        keep or sscope.close()
        keep or mscope.close()

        ln1b_s = E(nc.sbuf_tensor("ln1b_s", [P, DG, TOK], BF16, side="right"))
        acc_s = E(nc.sbuf_tensor("acc_s", [P, NTT, DM], F32, side="right"))

        lscope = contextlib.ExitStack()
        El = lscope.enter_context
        mrow_s = El(nc.sbuf_tensor("mrow_s", [1, TC], F32))
        rrow_s = El(nc.sbuf_tensor("rrow_s", [1, TC], F32))
        trow_s = El(nc.sbuf_tensor("trow_s", [1, TC], F32))
        sq_s = El(nc.sbuf_tensor("sq_s", [P, 2, TC], F32))

        v_rstdt = [0] * NTC
        sq_mm = [0, 0]
        v_bcast_prev = 0
        v_var_prev = 0
        for tc in range(NTC):
            t0 = tc * TC
            for dg in range(DG):
                @bvec
                def _(vector, dg=dg, t0=t0):
                    if dg >= 2:
                        sT.wait(vector, sq_mm[dg % 2])
                    vector.tensor_mul(sq_s[:, dg % 2, :],
                                      xT[:, dg, t0:t0 + TC],
                                      xT[:, dg, t0:t0 + TC])
                v_sq = sV.n

                @block.tensor
                def _(tensor, dg=dg, t0=t0, v_sq=v_sq):
                    sV.wait(tensor, v_sq)
                    if dg == 0:
                        bwait_free(tensor, 6)
                        bwait_free(tensor, 7)
                    tensor.matmul(pb[6][0:1, 0:TC], ones_all[:, 0:1],
                                  xT[:, dg, t0:t0 + TC],
                                  start=(dg == 0), stop=(dg == DG - 1))
                    mm = tensor.matmul(pb[7][0:1, 0:TC], ones_all[:, 0:1],
                                       sq_s[:, dg % 2, :],
                                       start=(dg == 0), stop=(dg == DG - 1))
                    sq_mm[dg % 2] = sT.inc(mm)
            v_stats = sT.n
            bank_prod[6] = (sT, v_stats)
            bank_prod[7] = (sT, v_stats)

            @bvec
            def _(vector, tc=tc, v_stats=v_stats, vb=v_bcast_prev):
                sT.wait(vector, v_stats)
                if tc > 0:
                    sT.wait(vector, vb)
                    sRS.wait(vector, v_rstdt[tc - 1])
                vector.tensor_scalar_mul(mrow_s[:], pb[6][0:1, 0:TC], 1.0 / DM)
                vector.tensor_scalar_mul(trow_s[:], pb[7][0:1, 0:TC], 1.0 / DM)
                vector.tensor_mul(rrow_s[:], mrow_s[:], mrow_s[:])
                vector.tensor_sub(trow_s[:], trow_s[:], rrow_s[:])
            v_var = sV.n
            bank_free[6] = (sV, v_var)
            bank_free[7] = (sV, v_var)

            @bscal
            def _(scalar, v_var=v_var):
                sV.wait(scalar, v_var)
                scalar.activation(out=trow_s[:], in_=trow_s[:],
                                  func=AF.Ln, bias=eps_s[0:1, 0:1])
                sA.inc(scalar.activation(out=rrow_s[:], in_=trow_s[:],
                                         func=AF.Exp, scale=-0.5))
            v_rstd = sA.n

            @block.tensor
            def _(tensor, tc=tc, v_rstd=v_rstd):
                sA.wait(tensor, v_rstd)
                bwait_free(tensor, 4)
                bwait_free(tensor, 5)
                tensor.matmul(pb[4][:, 0:TC], ones_all[0:1, :], mrow_s[:],
                              start=True, stop=True)
                mm = tensor.matmul(pb[5][:, 0:TC], ones_all[0:1, :], rrow_s[:],
                                   start=True, stop=True)
                sT.inc(mm)
            v_bcast = sT.n
            v_bcast_prev = v_bcast
            bank_prod[4] = (sT, v_bcast)
            bank_prod[5] = (sT, v_bcast)

            for dg in range(DG):
                @bvec
                def _(vector, dg=dg, t0=t0, v_bcast=v_bcast):
                    sT.wait(vector, v_bcast)
                    vector.tensor_sub(xT[:, dg, t0:t0 + TC],
                                      xT[:, dg, t0:t0 + TC], pb[4][:, 0:TC])
                    sV.inc(vector.tensor_tensor(
                        out=ln1b_s[:, dg, t0:t0 + TC],
                        in0=xT[:, dg, t0:t0 + TC], in1=pb[5][:, 0:TC],
                        op=ALU.mult))
            v_apply = sV.n
            bank_free[4] = (sV, v_apply)
            bank_free[5] = (sV, v_apply)

            @block.sync
            def _(sync, tc=tc, v_rstd=v_rstd):
                sA.wait(sync, v_rstd)
                sRS.inc(sync.dma_start(
                    out=rstd_d.ap()[tc * TC:(tc + 1) * TC].unsqueeze(0),
                    in_=rrow_s[0:1, :]), 16)
            v_rstdt[tc] = sRS.n
        v_ln1_v = sV.n

        @block.sync
        def _(sync):
            sRS.wait(sync, v_rstdt[NTC - 1])
            sRS.inc(sync.dma_start(
                out=rstdt_s[:],
                in_=rstd_d.ap().rearrange("(t p) -> p t", p=P),
                allow_slow_non_contiguous=True), 16)
        v_rstdt_load = sRS.n

        for tt in range(NTT):
            tc = (tt * P) // TC
            for dg in range(DG):
                @block.tensor
                def _(tensor, tt=tt, dg=dg):
                    bk = 2 + (dg % 2)
                    bwait_free(tensor, bk)
                    if tt == 0 and dg == 0:
                        sV.wait(tensor, v_ln1_v)
                    mm = tensor.transpose(pb[bk][:, 0:P],
                                          xT[:, dg, tt * P:(tt + 1) * P],
                                          id_s[:])
                    bank_prod[bk] = (sT, sT.inc(mm))

                @bvec
                def _(vector, tt=tt, dg=dg):
                    bk = 2 + (dg % 2)
                    bwait_prod(vector, bk)
                    if tt == 0 and dg == 0:
                        sRS.wait(vector, v_rstdt_load)
                    vector.tensor_single_scalar(
                        out=acc_s[:, tt, dg * P:(dg + 1) * P],
                        in_=pb[bk][:, 0:P], scalar=rstdt_s[:, tt:tt + 1],
                        op=ALU.mult)
                    bank_free[bk] = (sV, sV.n)

            @block.tensor
            def _(tensor, tt=tt):
                bwait_free(tensor, 6)
                for dg in range(DG):
                    tensor.matmul(pb[6][:, 0:NE], ln1b_s[:, dg, tt * P:(tt + 1) * P],
                                  wrb_s[:, dg, :], start=(dg == 0), stop=False)
                mm = tensor.matmul(pb[6][:, 0:NE], ones_all[0:1, :], br_s[:],
                                   start=False, stop=True)
                bank_prod[6] = (sT, sT.inc(mm))

            @bvec
            def _(vector):
                bwait_prod(vector, 6)
                sV.inc(vector.reduce_max(out=red_s[:, 0:1], in_=pb[6][:, 0:NE],
                                         axis=AX.X, negate=True))
            v_nm = sV.n

            @bscal
            def _(scalar, v_nm=v_nm):
                sV.wait(scalar, v_nm)
                e = scalar.activation(out=rscr_s[:, 0:NE], in_=pb[6][:, 0:NE],
                                      func=AF.Exp, bias=red_s[:, 0:1])
                bank_free[6] = (sA, sA.inc(e))
            v_e = sA.n

            @bvec
            def _(vector, tt=tt, v_e=v_e):
                sA.wait(vector, v_e)
                e = rscr_s[:, 0:NE]
                f = rscr_s[:, 8:8 + NE]
                w2 = rscr_s[:, 16:16 + NE]
                vector.reduce_max(out=red_s[:, 1:2], in_=e, axis=AX.X)
                vector.tensor_single_scalar(out=f, in_=e,
                                            scalar=red_s[:, 1:2],
                                            op=ALU.is_equal)
                vector.tensor_scalar(out=f, in0=f, scalar1=-1.0, scalar2=1.0,
                                     op0=ALU.mult, op1=ALU.add)
                vector.tensor_mul(w2, e, f)
                vector.reduce_max(out=red_s[:, 2:3], in_=w2, axis=AX.X)
                vector.tensor_single_scalar(out=f, in_=e,
                                            scalar=red_s[:, 2:3],
                                            op=ALU.is_ge)
                vector.tensor_mul(e, e, f)
                vector.reduce_sum(out=red_s[:, 3:4], in_=e, axis=AX.X)
                vector.reciprocal(out=red_s[:, 4:5], in_=red_s[:, 3:4])
                sV.inc(vector.tensor_scalar_mul(rtok_s[:, tt, :], e,
                                                red_s[:, 4:5]))
            v_rt = sV.n

            @block.tensor
            def _(tensor, tt=tt, v_rt=v_rt):
                bwait_free(tensor, 7)
                sV.wait(tensor, v_rt)
                mm = tensor.transpose(pb[7][0:NE, 0:P], rtok_s[:, tt, :],
                                      id_s[:])
                bank_prod[7] = (sT, sT.inc(mm))

            @bscal
            def _(scalar, tt=tt):
                bwait_prod(scalar, 7)
                cp = scalar.copy(out=rfm_s[:, tt * P:(tt + 1) * P],
                                 in_=pb[7][0:NE, 0:P])
                bank_free[7] = (sA, sA.inc(cp))
        v_accinit = sA.n
        v_rt_all = sV.n

        keep or lscope.close()
        keep or xT_scope.close()

        fscope = contextlib.ExitStack()
        Ef = fscope.enter_context
        stg = Ef(nc.sbuf_tensor("stg", [P, HQ], F32))
        w1qb = Ef(nc.sbuf_tensor("w1qb", [P, 2, DG, HQ], BF16))
        w2qb = Ef(nc.sbuf_tensor("w2qb", [P, 2, HC, DM], BF16))
        h_sb = Ef(nc.sbuf_tensor("h_sb", [P, HC, N5], BF16))
        out_s = Ef(nc.sbuf_tensor("out_s", [P, 2, DM], F32))

        NQT = NE * NQ
        W1B = (0, 1, 4, 5)
        EOB = (2, 3, 6, 7)
        v_w1_mmdone = {}
        v_w2_mmdone = {}
        v_castdone = {}
        v_b1 = {}
        v_b1_use = {}
        heread = {0: 0, 1: 0}
        cast_prev = [0]

        def emit_stream(QI):
            e, q = divmod(QI, NQ)
            slot = QI % 2
            if q == 0:
                @block.sync
                def _(sync, e=e, v=v_b1_use.get(e - 2, 0)):
                    if v:
                        sA.wait(sync, v)
                    sB1.inc(sync.dma_start(out=b1r_s[:, e % 2, :],
                                           in_=b1_h.ap()[e]), 16)
                v_b1[e] = sB1.n

            vfree = (v_w1_mmdone.get(QI - 2, 0), v_w2_mmdone.get(QI - 2, 0))
            for half in range(2):
                for ck in range(DG):
                    @block.sync
                    def _(sync, e=e, q=q, ck=ck, half=half, QI=QI,
                          vc=cast_prev[0]):
                        if QI == 0 and half == 0 and ck == 0:
                            sA.wait(sync, v_accinit)
                        if vc:
                            sG.wait(sync, vc)
                        if half == 0:
                            src_ap = w1_h.ap()[
                                e, ck * P:(ck + 1) * P,
                                q * HQ:(q + 1) * HQ]
                        else:
                            src_ap = w2_h.ap()[
                                e, q * HQ + ck * P:q * HQ + (ck + 1) * P, :]
                        sW1.inc(sync.dma_start(out=stg[:], in_=src_ap), 16)
                    v_dma = sW1.n

                    @bgps
                    def _(gpsimd, slot=slot, ck=ck, half=half, v_dma=v_dma,
                          vfree=vfree[half], first=(ck == 0)):
                        sW1.wait(gpsimd, v_dma)
                        if first and vfree:
                            sT.wait(gpsimd, vfree)
                        dst = (w1qb if half == 0 else w2qb)
                        gpsimd.tensor_copy(
                            out=dst[:, slot, ck, :], in_=stg[:])
                        cast_prev[0] = sG.n
            v_castdone[QI] = sG.n

        def emit_compute(QI):
            e, q = divmod(QI, NQ)
            slot = QI % 2
            b2eff = (e == 0 and q == 0)
            for t5 in range(NT5):
                t0 = t5 * N5
                gW = [0] * HC

                @block.tensor
                def _(tensor, e=e, q=q, QI=QI, t5=t5, slot=slot, t0=t0):
                    if t5 == 0:
                        sG.wait(tensor, v_castdone[QI])
                        if QI == 0:
                            sV.wait(tensor, v_ln1_v)
                    vrelu_base = sA.n
                    for hdc in range(HC):
                        bk = W1B[hdc % 4]
                        for dg in range(DG):
                            mm = tensor.matmul(
                                pb[bk][:, 0:N5],
                                w1qb[:, slot, dg, hdc * P:(hdc + 1) * P],
                                ln1b_s[:, dg, t0:t0 + N5],
                                start=(dg == 0), stop=(dg == DG - 1))
                            if dg == 0:
                                if hdc < 4:
                                    if bank_free[bk] is not None:
                                        c, v = bank_free[bk]
                                        mm._wait_ge(c.h, v)
                                else:
                                    mm._wait_ge(sA.h,
                                                vrelu_base + (hdc - 4) + 1)
                        gW[hdc] = sT.inc(mm)

                @bscal
                def _(scalar, e=e, q=q, vh=heread[0]):
                    for hdc in range(HC):
                        bk = W1B[hdc % 4]
                        if hdc == 0:
                            sB1.wait(scalar, v_b1[e])
                            if vh:
                                sT.wait(scalar, vh)
                        sT.wait(scalar, gW[hdc])
                        scalar.activation(
                            out=h_sb[:, hdc, :], in_=pb[bk][:, 0:N5],
                            func=AF.Relu,
                            bias=b1r_s[:, e % 2,
                                       q * HC + hdc:q * HC + hdc + 1])
                        bank_free[bk] = (sA, sA.n)
                v_hevac = sA.n
                v_w1_mmdone[QI] = sT.n
                mEO = [0] * (2 * T128)

                @block.tensor
                def _(tensor, e=e, QI=QI, t5=t5, slot=slot,
                      b2eff=b2eff, v_hevac=v_hevac):
                    sA.wait(tensor, v_hevac)
                    vstt_base = sV.n
                    for i in range(2 * T128):
                        t1, dh = divmod(i, 2)
                        ttg = t5 * T128 + t1
                        bk = EOB[i % 4]
                        for hdc in range(HC):
                            mm = tensor.matmul(
                                pb[bk][:, 0:512],
                                h_sb[:, hdc, t1 * P:(t1 + 1) * P],
                                w2qb[:, slot, hdc, dh * 512:(dh + 1) * 512],
                                start=(hdc == 0),
                                stop=(hdc == HC - 1 and not b2eff))
                            if hdc == 0:
                                if i < 4:
                                    if bank_free[bk] is not None:
                                        c, v = bank_free[bk]
                                        mm._wait_ge(c.h, v)
                                else:
                                    mm._wait_ge(sV.h,
                                                vstt_base + (i - 4) + 1)
                        if b2eff:
                            mm = tensor.matmul(
                                pb[bk][:, 0:512],
                                rfm_s[:, ttg * P:(ttg + 1) * P],
                                b2_s[:, dh * 512:(dh + 1) * 512],
                                start=False, stop=True)
                        mEO[i] = sT.inc(mm)
                heread[0] = sT.n
                v_w2_mmdone[QI] = sT.n

                @bvec
                def _(vector, e=e, QI=QI, t5=t5):
                    for i in range(2 * T128):
                        t1, dh = divmod(i, 2)
                        ttg = t5 * T128 + t1
                        bk = EOB[i % 4]
                        if i == 0 and QI == 0 and t5 == 0:
                            sA.wait(vector, v_accinit)
                            vector.wait_ge(sV.h, v_rt_all)
                        sT.wait(vector, mEO[i])
                        vector.scalar_tensor_tensor(
                            out=acc_s[:, ttg, dh * 512:(dh + 1) * 512],
                            in0=pb[bk][:, 0:512],
                            scalar=rtok_s[:, ttg, e:e + 1],
                            in1=acc_s[:, ttg, dh * 512:(dh + 1) * 512],
                            op0=ALU.mult, op1=ALU.add)
                        bank_free[bk] = (sV, sV.n)
            v_b1_use[e] = v_hevac

        emit_stream(0)
        for QI in range(NQT):
            if QI + 1 < NQT:
                emit_stream(QI + 1)
            emit_compute(QI)

        v_ffn_v = sV.n

        v_outdma = [0] * NTT
        for tt in range(NTT):
            @bvec
            def _(vector, tt=tt):
                if tt == 0:
                    sV.wait(vector, v_ffn_v)
                vector.reduce_sum(out=red_s[:, 0:1], in_=acc_s[:, tt, :],
                                  axis=AX.X)
                vector.tensor_scalar_mul(red_s[:, 1:2], red_s[:, 0:1],
                                         1.0 / DM)
                if tt >= 2:
                    sO[tt % 2].wait(vector, v_outdma[tt - 2])
                vector.scalar_tensor_tensor(
                    out=out_s[:, tt % 2, :], in0=acc_s[:, tt, :], scalar=1.0,
                    in1=acc_s[:, tt, :], op0=ALU.mult, op1=ALU.mult,
                    accum_out=red_s[:, 2:3])
                vector.tensor_scalar_mul(red_s[:, 3:4], red_s[:, 2:3],
                                         1.0 / DM)
                vector.tensor_mul(red_s[:, 4:5], red_s[:, 1:2], red_s[:, 1:2])
                sV.inc(vector.tensor_sub(red_s[:, 5:6], red_s[:, 3:4],
                                         red_s[:, 4:5]))
            v_var2 = sV.n

            @bscal
            def _(scalar, v_var2=v_var2):
                sV.wait(scalar, v_var2)
                scalar.activation(out=red_s[:, 6:7], in_=red_s[:, 5:6],
                                  func=AF.Ln, bias=eps_s[:, 0:1])
                sA.inc(scalar.activation(out=red_s[:, 6:7], in_=red_s[:, 6:7],
                                         func=AF.Exp, scale=-0.5))
            v_rstd2 = sA.n

            @bvec
            def _(vector, v_rstd2=v_rstd2):
                sA.wait(vector, v_rstd2)
                sV.inc(vector.scalar_tensor_tensor(
                    out=red_s[:, 7:8], in0=red_s[:, 1:2], scalar=-1.0,
                    in1=red_s[:, 6:7], op0=ALU.mult, op1=ALU.mult))
            v_nmr = sV.n

            @bvec
            def _(vector, tt=tt, v_nmr=v_nmr):
                vector.wait_ge(sV.h, v_nmr)
                nmb = red_s[:, 7:8].broadcast_to([P, DM])
                sV.inc(vector.scalar_tensor_tensor(
                    out=out_s[:, tt % 2, :], in0=acc_s[:, tt, :],
                    scalar=red_s[:, 6:7], in1=nmb,
                    op0=ALU.mult, op1=ALU.add))
            v_oev = sV.n

            @bgps
            def _(gpsimd, tt=tt, v_oev=v_oev):
                sV.wait(gpsimd, v_oev)
                sO[tt % 2].inc(gpsimd.dma_start(
                    out=out_h.ap()[tt * P:(tt + 1) * P, :],
                    in_=out_s[:, tt % 2, :]), 16)
            v_outdma[tt] = sO[tt % 2].n

        @bgps
        def _(gpsimd):
            sO[0].wait(gpsimd)
            sO[1].wait(gpsimd)

        keep or fscope.close()
        if keep:
            for sc in (fscope, lscope, sscope, xscope, mscope, xT_scope):
                sc.close()

    return nc


def prepare_in_maps(L0_attn_x, L0_attn_A_log, L0_attn_D, L0_attn_Wx,
                    L0_attn_dt_W, L0_attn_dt_b, L0_ffn_W1, L0_ffn_b1,
                    L0_ffn_W2, L0_ffn_b2, L0_ffn_Wr, L0_ffn_br):
    CL = L0_attn_x.shape[1] // NCORE
    x = np.ascontiguousarray(L0_attn_x, dtype=np.float32)
    c = np.ascontiguousarray

    def pd(v):
        v = np.asarray(v, dtype=np.float32)
        return c(v.reshape(DG, P, *v.shape[1:]).transpose(
            1, 0, *range(2, v.ndim + 1)))

    shared = {
        "WxT": c(np.asarray(L0_attn_Wx, np.float32).T),
        "dtWT": c(np.asarray(L0_attn_dt_W, np.float32).T),
        "A_pd": pd(L0_attn_A_log),
        "D_pd": pd(L0_attn_D),
        "dtb_pd": pd(L0_attn_dt_b),
        "ident": np.eye(P, dtype=np.float32),
        "Wr": c(np.asarray(L0_ffn_Wr, np.float32)),
        "br_r": c(np.asarray(L0_ffn_br, np.float32).reshape(1, NE)),
        "b2": c(np.asarray(L0_ffn_b2, np.float32)),
        "b1r": c(np.asarray(L0_ffn_b1, np.float32).reshape(
            NE, H // P, P).transpose(0, 2, 1)),
        "W1": c(np.asarray(L0_ffn_W1, np.float32)),
        "W2": c(np.asarray(L0_ffn_W2, np.float32)),
    }

    in_maps = []
    for cid in range(NCORE):
        m = dict(shared)
        m["x01"] = c(np.concatenate(
            [x[0, cid * CL:(cid + 1) * CL], x[1, cid * CL:(cid + 1) * CL]],
            axis=0))
        m["maskv"] = c(np.broadcast_to(
            (np.arange(NCORE) < cid).astype(np.float32), (P, NCORE)))
        in_maps.append(m)
    return in_maps, CL


def assemble_out(results, CL):
    out = np.empty((NB, NCORE * CL, DM), dtype=np.float32)
    for cid in range(NCORE):
        o = results[cid]["out"]
        out[0, cid * CL:(cid + 1) * CL] = o[:CL]
        out[1, cid * CL:(cid + 1) * CL] = o[CL:]
    return out


def kernel(L0_attn_x, L0_attn_A_log, L0_attn_D, L0_attn_Wx, L0_attn_dt_W,
           L0_attn_dt_b, L0_ffn_W1, L0_ffn_b1, L0_ffn_W2, L0_ffn_b2,
           L0_ffn_Wr, L0_ffn_br):
    in_maps, CL = prepare_in_maps(
        L0_attn_x, L0_attn_A_log, L0_attn_D, L0_attn_Wx, L0_attn_dt_W,
        L0_attn_dt_b, L0_ffn_W1, L0_ffn_b1, L0_ffn_W2, L0_ffn_b2,
        L0_ffn_Wr, L0_ffn_br)
    nc = build_nc(CL=CL)
    res = run_bass_kernel_spmd(nc, in_maps, core_ids=list(range(NCORE)))
    return assemble_out(res.results, CL)


# version 10 (final)
# speedup vs baseline: 1.1479x; 1.0193x over previous
import contextlib
import numpy as np
import concourse.bass as bass
import concourse.mybir as mybir
from concourse.bass_utils import run_bass_kernel_spmd

F32 = mybir.dt.float32
BF16 = mybir.dt.bfloat16
AF = mybir.ActivationFunctionType
ALU = mybir.AluOpType
AX = mybir.AxisListType

P = 128
DM = 1024
DG = 8
S = 16
R = 64
NB = 2
NCORE = 8
H = 4096
NE = 8
NQ = 4
HQ = H // NQ
HC = HQ // P
EPS = 1e-5


class Ctr:
    def __init__(self, ctx, nc, name):
        self.h = ctx.enter_context(nc.semaphore(name))
        self.n = 0
        self.seen = {}

    def inc(self, instr, k=1):
        if id(instr) in self.seen:
            return self.n
        instr.then_inc(self.h, k)
        self.n += k
        return self.n

    def wait(self, eng, at=None):
        eng.wait_ge(self.h, self.n if at is None else at)


class ChainEng:
    _PASS = frozenset(("wait_ge", "drain"))
    _NOAUTO = frozenset(("dma_start", "collective_compute"))

    def __init__(self, eng, ctr):
        self._e = eng
        self._c = ctr
        self._skip = 0

    def __getattr__(self, name):
        f = getattr(self._e, name)
        if name in ChainEng._PASS or name in ChainEng._NOAUTO \
                or not callable(f):
            return f
        c = self._c
        me = self

        def g(*a, **k):
            n0 = c.n
            ins = f(*a, **k)
            if me._skip > 0:
                me._skip -= 1
            elif n0:
                ins._wait_ge(c.h, n0)
            c.inc(ins)
            c.seen[id(ins)] = ins
            return ins
        return g


def chained_block(deco, ctr):
    def d(fn):
        def inner(eng):
            return fn(ChainEng(eng, ctr))
        return deco(inner)
    return d


def build_nc(CL=1024, TC=256, keep=False):
    TOK = NB * CL
    NTC = TOK // TC
    TCB = CL // TC
    NTT = TOK // P
    N5 = min(512, TOK)
    NT5 = TOK // N5
    T128 = N5 // P
    PACKC = 2 * P + 2 * DG

    nc = bass.Bass("TRN2", target_bir_lowering=False)

    x_h = nc.dram_tensor("x01", [TOK, DM], F32, kind="ExternalInput")
    wxt_h = nc.dram_tensor("WxT", [DM, 96], F32, kind="ExternalInput")
    dtwt_h = nc.dram_tensor("dtWT", [R, DM], F32, kind="ExternalInput")
    a_h = nc.dram_tensor("A_pd", [P, DG, S], F32, kind="ExternalInput")
    d_h = nc.dram_tensor("D_pd", [P, DG], F32, kind="ExternalInput")
    dtb_h = nc.dram_tensor("dtb_pd", [P, DG], F32, kind="ExternalInput")
    mask_h = nc.dram_tensor("maskv", [P, NCORE], F32, kind="ExternalInput")
    id_h = nc.dram_tensor("ident", [P, P], F32, kind="ExternalInput")
    wr_h = nc.dram_tensor("Wr", [DM, NE], F32, kind="ExternalInput")
    br_h = nc.dram_tensor("br_r", [1, NE], F32, kind="ExternalInput")
    b2_h = nc.dram_tensor("b2", [NE, DM], F32, kind="ExternalInput")
    b1_h = nc.dram_tensor("b1r", [NE, P, H // P], F32, kind="ExternalInput")
    w1_h = nc.dram_tensor("W1", [NE, DM, H], F32, kind="ExternalInput")
    w2_h = nc.dram_tensor("W2", [NE, H, DM], F32, kind="ExternalInput")
    out_h = nc.dram_tensor("out", [TOK, DM], F32, kind="ExternalOutput")
    cci_h = nc.dram_tensor("cc_in", [P * PACKC], F32, kind="Internal")
    rstd_d = nc.dram_tensor("rstd_d", [NB * CL], F32, kind="Internal")
    cco_h = nc.dram_tensor("cc_out", [NCORE * P * PACKC], F32, kind="Internal")

    ctx = contextlib.ExitStack()
    with ctx:
        E = ctx.enter_context

        sX = Ctr(ctx, nc, "sX")
        sXD = [Ctr(ctx, nc, "sXD0"), Ctr(ctx, nc, "sXD1")]
        sR = Ctr(ctx, nc, "sR")
        sB1 = Ctr(ctx, nc, "sB1")
        sRS = Ctr(ctx, nc, "sRS")
        sW1 = Ctr(ctx, nc, "sW1")
        sW2 = Ctr(ctx, nc, "sW2")
        sT = Ctr(ctx, nc, "sT")
        sA = Ctr(ctx, nc, "sA")
        sV = Ctr(ctx, nc, "sV")
        sG = Ctr(ctx, nc, "sG")
        sC = Ctr(ctx, nc, "sC")
        sCC = Ctr(ctx, nc, "sCC")
        sO = [Ctr(ctx, nc, "sO0"), Ctr(ctx, nc, "sO1")]

        id_s = E(nc.sbuf_tensor("id_s", [P, P], F32))
        ones_all = E(nc.sbuf_tensor("ones_all", [P, P], F32))
        a_s = E(nc.sbuf_tensor("a_s", [P, DG, S], F32))
        dp1_s = E(nc.sbuf_tensor("dp1_s", [P, DG], F32))
        dtb_s = E(nc.sbuf_tensor("dtb_s", [P, DG], F32))
        mask_s = E(nc.sbuf_tensor("mask_s", [P, NCORE], F32))
        wxtb_s = E(nc.sbuf_tensor("wxtb_s", [P, DG, 96], BF16))
        dtwtb_s = E(nc.sbuf_tensor("dtwtb_s", [R, DM], BF16))
        wrb_s = E(nc.sbuf_tensor("wrb_s", [P, DG, NE], BF16))
        br_s = E(nc.sbuf_tensor("br_s", [1, NE], F32))
        b2_s = E(nc.sbuf_tensor("b2_s", [NE, DM], F32))
        b1r_s = E(nc.sbuf_tensor("b1r_s", [P, 2, H // P], F32))
        zeros_s = E(nc.sbuf_tensor("zeros_s", [P, TC], F32))
        state_s = E(nc.sbuf_tensor("state_s", [P, DG, S], F32))
        sstate_s = E(nc.sbuf_tensor("sstate_s", [P, DG], F32))
        ssum_s = E(nc.sbuf_tensor("ssum_s", [P, NB, DG], F32))
        hT_s = E(nc.sbuf_tensor("hT_s", [P, NB, P], F32))
        hin_s = E(nc.sbuf_tensor("hin_s", [P, NB, DG, S], F32))
        red_s = E(nc.sbuf_tensor("red_s", [P, 8], F32))
        rscr_s = E(nc.sbuf_tensor("rscr_s", [P, 24], F32))
        rfm_s = E(nc.sbuf_tensor("rfm_s", [NE, TOK], F32))
        rtok_s = E(nc.sbuf_tensor("rtok_s", [P, NTT, NE], F32))
        rstdt_s = E(nc.sbuf_tensor("rstdt_s", [P, NTT], F32))
        edg_s = E(nc.sbuf_tensor("edg_s", [P, DG, S], F32))
        tdg_s = E(nc.sbuf_tensor("tdg_s", [P, DG], F32))
        thp_s = E(nc.sbuf_tensor("thp_s", [P, P], F32))
        eps_s = E(nc.sbuf_tensor("eps_s", [P, 1], F32))

        pb = [E(nc.psum_tensor(f"pb{i}", [P, 512], F32)) for i in range(8)]
        bank_free = [None] * 8
        bank_prod = [None] * 8

        def bwait_free(eng, bk):
            if bank_free[bk] is not None:
                c, v = bank_free[bk]
                c.wait(eng, v)

        def bwait_prod(eng, bk):
            c, v = bank_prod[bk]
            c.wait(eng, v)

        block = E(nc.Block())
        bvec = chained_block(block.vector, sV)
        bscal = chained_block(block.scalar, sA)
        bgps = chained_block(block.gpsimd, sG)

        xT_scope = contextlib.ExitStack()
        xT = xT_scope.enter_context(nc.sbuf_tensor("xT", [P, DG, TOK], F32))

        mscope = contextlib.ExitStack()
        Em = mscope.enter_context
        wxts_s = Em(nc.sbuf_tensor("wxts_s", [P, DG, 96], F32))
        dtwts_s = Em(nc.sbuf_tensor("dtwts_s", [R, DM], F32))
        wrs_s = Em(nc.sbuf_tensor("wrs_s", [P, DG, NE], F32))
        proj_s = Em(nc.sbuf_tensor("proj_s", [96, TOK], F32))
        projb_s = Em(nc.sbuf_tensor("projb_s", [R, TOK], BF16))
        gath_s = Em(nc.sbuf_tensor("gath_s", [P, NCORE, PACKC], F32))
        pack_s = Em(nc.sbuf_tensor("pack_s", [P, PACKC], F32))

        @block.sync
        def _(sync):
            for dst, src in ((id_s, id_h), (a_s, a_h), (dp1_s, d_h),
                             (dtb_s, dtb_h), (mask_s, mask_h), (br_s, br_h),
                             (b2_s, b2_h)):
                sX.inc(sync.dma_start(out=dst[:], in_=src.ap()), 16)
            sX.inc(sync.dma_start(
                out=wxts_s[:],
                in_=wxt_h.ap().rearrange("(g p) m -> p g m", p=P)), 16)
            sX.inc(sync.dma_start(out=dtwts_s[:], in_=dtwt_h.ap()), 16)
            sX.inc(sync.dma_start(
                out=wrs_s[:],
                in_=wr_h.ap().rearrange("(g p) e -> p g e", p=P)), 16)
        v_const = sX.n

        @bscal
        def _(scalar):
            sX.wait(scalar, v_const)
            sA.inc(scalar.activation(out=a_s[:], in_=a_s[:], func=AF.Exp))
        v_aexp = sA.n

        @bvec
        def _(vector):
            sX.wait(vector, v_const)
            sA.wait(vector, v_aexp)
            vector.tensor_scalar_mul(a_s[:], a_s[:], -1.0)
            vector.tensor_scalar_add(dp1_s[:], dp1_s[:], 1.0)
            vector.memset(ones_all[:], 1.0)
            vector.memset(zeros_s[:], 0.0)
            vector.memset(state_s[:], 0.0)
            vector.memset(sstate_s[:], 0.0)
            vector.memset(ssum_s[:], 0.0)
            vector.memset(hin_s[:], 0.0)
            vector.memset(eps_s[:], EPS)
            vector.tensor_copy(out=wxtb_s[:], in_=wxts_s[:])
            vector.tensor_copy(out=dtwtb_s[:], in_=dtwts_s[:])
            sV.inc(vector.tensor_copy(out=wrb_s[:], in_=wrs_s[:]))
        v_setup = sV.n

        xscope = contextlib.ExitStack()
        Ex = xscope.enter_context
        xstage = Ex(nc.sbuf_tensor("xstage", [P, 2, DM], F32))
        xTb = Ex(nc.sbuf_tensor("xTb", [P, DG, TOK], BF16))

        evac_done = [0] * NTT
        for tt in range(NTT):
            @block.sync
            def _(sync, tt=tt):
                if tt >= 2:
                    sV.wait(sync, evac_done[tt - 2])
                sXD[tt % 2].inc(sync.dma_start(
                    out=xstage[:, tt % 2, :],
                    in_=x_h.ap()[tt * P:(tt + 1) * P, :]), 16)
            v_x = sXD[tt % 2].n

            for dg in range(DG):
                @block.tensor
                def _(tensor, tt=tt, dg=dg, v_x=v_x):
                    bk = 2 + (dg % 2)
                    bwait_free(tensor, bk)
                    if dg == 0:
                        sXD[tt % 2].wait(tensor, v_x)
                        if tt == 0:
                            sV.wait(tensor, v_setup)
                    mm = tensor.transpose(
                        pb[bk][:, 0:P], xstage[:, tt % 2, dg * P:(dg + 1) * P],
                        id_s[:])
                    bank_prod[bk] = (sT, sT.inc(mm))

                @bvec
                def _(vector, tt=tt, dg=dg):
                    bk = 2 + (dg % 2)
                    bwait_prod(vector, bk)
                    vector.tensor_copy(
                        out=xT[:, dg, tt * P:(tt + 1) * P], in_=pb[bk][:, 0:P])
                    cp = vector.tensor_copy(
                        out=xTb[:, dg, tt * P:(tt + 1) * P],
                        in_=pb[bk][:, 0:P])
                    bank_free[bk] = (sV, sV.inc(cp))
            evac_done[tt] = sV.n
        v_xready = sV.n

        for nt in range(TOK // N5):
            @block.tensor
            def _(tensor, nt=nt):
                if nt == 0:
                    sV.wait(tensor, v_xready)
                bk = 4 + (nt % 2)
                bwait_free(tensor, bk)
                for dg in range(DG):
                    mm = tensor.matmul(
                        pb[bk][0:96, 0:N5], wxtb_s[:, dg, :],
                        xTb[:, dg, nt * N5:(nt + 1) * N5],
                        start=(dg == 0), stop=(dg == DG - 1))
                bank_prod[bk] = (sT, sT.inc(mm))

            @bscal
            def _(scalar, nt=nt):
                bk = 4 + (nt % 2)
                bwait_prod(scalar, bk)
                cp = scalar.copy(out=proj_s[:, nt * N5:(nt + 1) * N5],
                                 in_=pb[bk][0:96, 0:N5])
                bank_free[bk] = (sA, sA.inc(cp))
        v_proj = sA.n

        @bvec
        def _(vector):
            sA.wait(vector, v_proj)
            sV.inc(vector.tensor_copy(out=projb_s[:], in_=proj_s[0:R, :]))
        v_projb = sV.n
        keep or xscope.close()

        SH = S // 2
        sscope = contextlib.ExitStack()
        Es = sscope.enter_context
        dtall_s = Es(nc.sbuf_tensor("dtall_s", [P, DG, TC], F32))
        aux_s = Es(nc.sbuf_tensor("aux_s", [P, DG, TC], F32))
        bbc_s = Es(nc.sbuf_tensor("bbc_s", [P, TC, SH], F32))
        cbc_s = Es(nc.sbuf_tensor("cbc_s", [P, TC, SH], F32))
        da_s = Es(nc.sbuf_tensor("da_s", [P, 2, TC, SH], F32))
        u_s = Es(nc.sbuf_tensor("u_s", [P, TC, SH], F32))
        hh_s = Es(nc.sbuf_tensor("hh_s", [P, TC, SH], F32))
        dt2_s = Es(nc.sbuf_tensor("dt2_s", [P, TC], F32))
        da_prev = {0: 0, 1: 0}
        y_s = Es(nc.sbuf_tensor("y_s", [P, TC], F32))
        brow_s = Es(nc.sbuf_tensor("brow_s", [1, SH * TC], F32))
        crow_s = Es(nc.sbuf_tensor("crow_s", [1, SH * TC], F32))

        rows_waw = {"b": 0, "c": 0}

        def emit_rows(tc, sh, want_b, v_reader_done):
            t0 = tc * TC
            kinds = (["b"] if want_b else []) + ["c"]

            @block.sync
            def _(sync, tc=tc, sh=sh, kinds=tuple(kinds),
                  w_b=rows_waw["b"], w_c=rows_waw["c"]):
                sA.wait(sync, v_proj)
                for kind in kinds:
                    w = w_b if kind == "b" else w_c
                    if w:
                        sT.wait(sync, w)
                    base = 64 + sh * SH if kind == "b" else 80 + sh * SH
                    dst = brow_s if kind == "b" else crow_s
                    sR.inc(sync.dma_start(
                        out=dst[0:1, :],
                        in_=proj_s[base:base + SH, t0:t0 + TC]), 16)
            v_rows = sR.n

            for i, kind in enumerate(kinds):
                for si in range(SH):
                    bk = (i * SH + si) % 2
                    srcrow = brow_s if kind == "b" else crow_s

                    @block.tensor
                    def _(tensor, bk=bk, si=si, srcrow=srcrow, kind=kind,
                          first=(i == 0 and si == 0), v_rows=v_rows):
                        bwait_free(tensor, bk)
                        if first:
                            sR.wait(tensor, v_rows)
                        mm = tensor.matmul(
                            pb[bk][:, 0:TC], ones_all[0:1, :],
                            srcrow[0:1, si * TC:(si + 1) * TC],
                            start=True, stop=True)
                        bank_prod[bk] = (sT, sT.inc(mm))
                        rows_waw[kind] = sT.n

                    @bvec
                    def _(vector, bk=bk, si=si, kind=kind):
                        bwait_prod(vector, bk)
                        dst = bbc_s if kind == "b" else cbc_s
                        vector.tensor_copy(out=dst[:, :, si], in_=pb[bk][:, 0:TC])
                        bank_free[bk] = (sV, sV.n)
            return sV.n

        def emit_dt(tc, dg, v_waw_vec, phase, b, first):
            t0 = tc * TC

            @block.tensor
            def _(tensor):
                bwait_free(tensor, 2)
                if tc == 0 and dg == 0:
                    sV.wait(tensor, v_projb)
                mm = tensor.matmul(
                    pb[2][:, 0:TC], dtwtb_s[:, dg * P:(dg + 1) * P],
                    projb_s[:, t0:t0 + TC], start=True, stop=True)
                bank_prod[2] = (sT, sT.inc(mm))

            @bscal
            def _(scalar):
                bwait_prod(scalar, 2)
                if v_waw_vec:
                    sV.wait(scalar, v_waw_vec)
                bias = dtb_s[:, dg:dg + 1]
                scalar.activation(out=dt2_s[:], in_=pb[2][:, 0:TC],
                                  func=AF.Abs, bias=bias)
                scalar.activation(out=dt2_s[:], in_=dt2_s[:],
                                  func=AF.Exp, scale=-1.0)
                scalar.activation(out=dt2_s[:], in_=dt2_s[:],
                                  func=AF.Ln, bias=1.0)
                rl = scalar.activation(out=dtall_s[:, dg, :], in_=pb[2][:, 0:TC],
                                       func=AF.Relu, bias=bias)
                bank_free[2] = (sA, sA.inc(rl))
            v_sp = sA.n

            @bvec
            def _(vector, v_sp=v_sp, dg=dg, b=b, first=first, phase=phase,
                  v_waw_vec=v_waw_vec):
                if v_waw_vec:
                    vector.wait_ge(sV.h, v_waw_vec)
                sA.wait(vector, v_sp)
                dtv = dtall_s[:, dg, :]
                vector.tensor_add(dtv, dtv, dt2_s[:])
                if phase == "A":
                    vector.reduce_sum(out=red_s[:, 0:1], in_=dtv, axis=AX.X)
                    vector.tensor_add(ssum_s[:, b, dg:dg + 1],
                                      ssum_s[:, b, dg:dg + 1], red_s[:, 0:1])
                    f = vector.tensor_mul(aux_s[:, dg, :], dtv,
                                          xT[:, dg, t0:t0 + TC])
                else:
                    init = 0.0 if first else sstate_s[:, dg:dg + 1]
                    vector.tensor_tensor_scan(
                        out=aux_s[:, dg, :], data0=dtv, data1=zeros_s[:],
                        initial=init, op0=ALU.add, op1=ALU.add)
                    f = vector.tensor_copy(out=sstate_s[:, dg:dg + 1],
                                           in_=aux_s[:, dg, TC - 1:TC])
                sV.inc(f)
            return sV.n

        v_vec_done = 0
        for tc in range(NTC):
            b = tc // TCB
            first = (tc % TCB == 0)
            last = (tc % TCB == TCB - 1)
            t0 = tc * TC
            for dg in range(DG):
                emit_dt(tc, dg, v_vec_done, "A", b, first)
                v_vec_done = sV.n
            v_dtall = sV.n

            for sh in range(2):
                v_bc = emit_rows(tc, sh, True, v_vec_done)
                for dg in range(DG):
                    par = dg % 2

                    @bscal
                    def _(scalar, dg=dg, sh=sh, par=par,
                          v_w=max(v_dtall, da_prev[par])):
                        sV.wait(scalar, v_w)
                        for si in range(SH):
                            scalar.activation(
                                out=da_s[:, par, :, si], in_=dtall_s[:, dg, :],
                                func=AF.Exp,
                                scale=a_s[:, dg, sh * SH + si:sh * SH + si + 1])
                    v_da = sA.n

                    @bvec
                    def _(vector, dg=dg, sh=sh, t0=t0, first=first, par=par,
                          v_da=v_da):
                        dtxb = aux_s[:, dg, :].unsqueeze(2).broadcast_to(
                            [P, TC, SH])
                        vector.tensor_tensor(out=u_s[:], in0=bbc_s[:],
                                             in1=dtxb, op=ALU.mult)
                        sA.wait(vector, v_da)
                        vector.wait_ge(sV.h, sV.n)
                        vector._skip = SH
                        for si in range(SH):
                            sg = sh * SH + si
                            init = 0.0 if first else state_s[:, dg, sg:sg + 1]
                            vector.tensor_tensor_scan(
                                out=hh_s[:, :, si], data0=da_s[:, par, :, si],
                                data1=u_s[:, :, si], initial=init,
                                op0=ALU.mult, op1=ALU.add)
                        vector.tensor_copy(
                            out=state_s[:, dg, sh * SH:(sh + 1) * SH],
                            in_=hh_s[:, TC - 1, :])
                        vector.tensor_mul(hh_s[:], hh_s[:], cbc_s[:])
                        vector.reduce_sum(out=y_s[:], in_=hh_s[:], axis=AX.X)
                        if sh == 0:
                            f = vector.scalar_tensor_tensor(
                                out=xT[:, dg, t0:t0 + TC],
                                in0=xT[:, dg, t0:t0 + TC],
                                scalar=dp1_s[:, dg:dg + 1], in1=y_s[:],
                                op0=ALU.mult, op1=ALU.add)
                        else:
                            f = vector.tensor_add(
                                xT[:, dg, t0:t0 + TC],
                                xT[:, dg, t0:t0 + TC], y_s[:])
                        sV.inc(f)
                    v_vec_done = sV.n
                    da_prev[par] = sV.n

            if last:
                @bvec
                def _(vector, b=b):
                    sV.inc(vector.tensor_copy(
                        out=hT_s[:, b, :],
                        in_=state_s[:].rearrange("p g s -> p (g s)")))
                v_vec_done = sV.n

        @bvec
        def _(vector, v_sw=sV.n):
            vector.wait_ge(sV.h, v_sw)
            vector.tensor_copy(out=pack_s[:, 0:P], in_=hT_s[:, 0, :])
            vector.tensor_copy(out=pack_s[:, P:2 * P], in_=hT_s[:, 1, :])
            sV.inc(vector.tensor_copy(
                out=pack_s[:, 2 * P:2 * P + 2 * DG],
                in_=ssum_s[:].rearrange("p b g -> p (b g)")))
        v_pack = sV.n

        @bgps
        def _(gpsimd):
            sV.wait(gpsimd, v_pack)
            sC.inc(gpsimd.dma_start(
                out=cci_h.ap().rearrange("(p c) -> p c", p=P),
                in_=pack_s[:]), 16)
            sC.wait(gpsimd)
            sCC.inc(gpsimd.collective_compute(
                "AllGather", ALU.bypass, replica_groups=[list(range(NCORE))],
                ins=[cci_h.ap()], outs=[cco_h.ap()]), 1)
            sCC.wait(gpsimd)
            sC.inc(gpsimd.dma_start(
                out=gath_s[:],
                in_=cco_h.ap().rearrange("(r p c) -> p r c", p=P, r=NCORE)), 16)
        v_gath = sC.n

        for b in range(NB):
            for j in range(NCORE - 1):
                @bvec
                def _(vector, b=b, j=j):
                    if b == 0 and j == 0:
                        sC.wait(vector, v_gath)
                    vector.tensor_scalar_mul(
                        tdg_s[:],
                        gath_s[:, j, 2 * P + b * DG:2 * P + (b + 1) * DG],
                        mask_s[:, j:j + 1])
                    smb = tdg_s[:].unsqueeze(2).broadcast_to([P, DG, S])
                    sV.inc(vector.tensor_tensor(out=edg_s[:], in0=a_s[:],
                                                in1=smb, op=ALU.mult))
                v_1 = sV.n

                @bscal
                def _(scalar, v_1=v_1):
                    sV.wait(scalar, v_1)
                    sA.inc(scalar.activation(out=edg_s[:], in_=edg_s[:],
                                             func=AF.Exp))
                v_2 = sA.n

                @bvec
                def _(vector, b=b, j=j, v_2=v_2, v_sw=sV.n):
                    vector.wait_ge(sV.h, v_sw)
                    sA.wait(vector, v_2)
                    hv = hin_s[:, b, :, :].rearrange("p g s -> p (g s)")
                    ev = edg_s[:].rearrange("p g s -> p (g s)")
                    vector.tensor_mul(hv, hv, ev)
                    vector.tensor_scalar_mul(
                        thp_s[:], gath_s[:, j, b * P:(b + 1) * P],
                        mask_s[:, j:j + 1])
                    sV.inc(vector.tensor_add(hv, hv, thp_s[:]))
        v_vec_done = sV.n

        for tc in range(NTC):
            b = tc // TCB
            first = (tc % TCB == 0)
            t0 = tc * TC
            for dg in range(DG):
                emit_dt(tc, dg, v_vec_done, "B", b, first)
                v_vec_done = sV.n
            v_sall = sV.n

            for sh in range(2):
                v_bc = emit_rows(tc, sh, False, v_vec_done)
                for dg in range(DG):
                    par = dg % 2

                    @bscal
                    def _(scalar, dg=dg, sh=sh, par=par,
                          v_w=max(v_sall, da_prev[par])):
                        sV.wait(scalar, v_w)
                        for si in range(SH):
                            scalar.activation(
                                out=da_s[:, par, :, si], in_=aux_s[:, dg, :],
                                func=AF.Exp,
                                scale=a_s[:, dg, sh * SH + si:sh * SH + si + 1])
                    v_e = sA.n

                    @bvec
                    def _(vector, b=b, dg=dg, sh=sh, t0=t0, par=par,
                          v_e=v_e):
                        sA.wait(vector, v_e)
                        vector.wait_ge(sV.h, sV.n)
                        vector._skip = SH
                        for si in range(SH):
                            sg = sh * SH + si
                            vector.scalar_tensor_tensor(
                                out=u_s[:, :, si], in0=da_s[:, par, :, si],
                                scalar=hin_s[:, b, dg, sg:sg + 1],
                                in1=cbc_s[:, :, si], op0=ALU.mult, op1=ALU.mult)
                        vector.reduce_sum(out=y_s[:], in_=u_s[:], axis=AX.X)
                        sV.inc(vector.tensor_add(
                            xT[:, dg, t0:t0 + TC], xT[:, dg, t0:t0 + TC],
                            y_s[:]))
                    v_vec_done = sV.n
                    da_prev[par] = sV.n
        v_phaseB_v = sV.n

        keep or sscope.close()
        keep or mscope.close()

        ln1b_s = E(nc.sbuf_tensor("ln1b_s", [P, DG, TOK], BF16, side="right"))
        acc_s = E(nc.sbuf_tensor("acc_s", [P, NTT, DM], F32, side="right"))

        lscope = contextlib.ExitStack()
        El = lscope.enter_context
        mrow_s = El(nc.sbuf_tensor("mrow_s", [1, TC], F32))
        rrow_s = El(nc.sbuf_tensor("rrow_s", [1, TC], F32))
        trow_s = El(nc.sbuf_tensor("trow_s", [1, TC], F32))
        sq_s = El(nc.sbuf_tensor("sq_s", [P, 2, TC], F32))

        v_rstdt = [0] * NTC
        sq_mm = [0, 0]
        v_bcast_prev = 0
        v_var_prev = 0
        for tc in range(NTC):
            t0 = tc * TC
            for dg in range(DG):
                @bvec
                def _(vector, dg=dg, t0=t0):
                    if dg >= 2:
                        sT.wait(vector, sq_mm[dg % 2])
                    vector.tensor_mul(sq_s[:, dg % 2, :],
                                      xT[:, dg, t0:t0 + TC],
                                      xT[:, dg, t0:t0 + TC])
                v_sq = sV.n

                @block.tensor
                def _(tensor, dg=dg, t0=t0, v_sq=v_sq):
                    sV.wait(tensor, v_sq)
                    if dg == 0:
                        bwait_free(tensor, 6)
                        bwait_free(tensor, 7)
                    tensor.matmul(pb[6][0:1, 0:TC], ones_all[:, 0:1],
                                  xT[:, dg, t0:t0 + TC],
                                  start=(dg == 0), stop=(dg == DG - 1))
                    mm = tensor.matmul(pb[7][0:1, 0:TC], ones_all[:, 0:1],
                                       sq_s[:, dg % 2, :],
                                       start=(dg == 0), stop=(dg == DG - 1))
                    sq_mm[dg % 2] = sT.inc(mm)
            v_stats = sT.n
            bank_prod[6] = (sT, v_stats)
            bank_prod[7] = (sT, v_stats)

            @bvec
            def _(vector, tc=tc, v_stats=v_stats, vb=v_bcast_prev):
                sT.wait(vector, v_stats)
                if tc > 0:
                    sT.wait(vector, vb)
                    sRS.wait(vector, v_rstdt[tc - 1])
                vector.tensor_scalar_mul(mrow_s[:], pb[6][0:1, 0:TC], 1.0 / DM)
                vector.tensor_scalar_mul(trow_s[:], pb[7][0:1, 0:TC], 1.0 / DM)
                vector.tensor_mul(rrow_s[:], mrow_s[:], mrow_s[:])
                vector.tensor_sub(trow_s[:], trow_s[:], rrow_s[:])
            v_var = sV.n
            bank_free[6] = (sV, v_var)
            bank_free[7] = (sV, v_var)

            @bscal
            def _(scalar, v_var=v_var):
                sV.wait(scalar, v_var)
                scalar.activation(out=trow_s[:], in_=trow_s[:],
                                  func=AF.Ln, bias=eps_s[0:1, 0:1])
                sA.inc(scalar.activation(out=rrow_s[:], in_=trow_s[:],
                                         func=AF.Exp, scale=-0.5))
            v_rstd = sA.n

            @block.tensor
            def _(tensor, tc=tc, v_rstd=v_rstd):
                sA.wait(tensor, v_rstd)
                bwait_free(tensor, 4)
                bwait_free(tensor, 5)
                tensor.matmul(pb[4][:, 0:TC], ones_all[0:1, :], mrow_s[:],
                              start=True, stop=True)
                mm = tensor.matmul(pb[5][:, 0:TC], ones_all[0:1, :], rrow_s[:],
                                   start=True, stop=True)
                sT.inc(mm)
            v_bcast = sT.n
            v_bcast_prev = v_bcast
            bank_prod[4] = (sT, v_bcast)
            bank_prod[5] = (sT, v_bcast)

            for dg in range(DG):
                @bvec
                def _(vector, dg=dg, t0=t0, v_bcast=v_bcast):
                    sT.wait(vector, v_bcast)
                    vector.tensor_sub(xT[:, dg, t0:t0 + TC],
                                      xT[:, dg, t0:t0 + TC], pb[4][:, 0:TC])
                    sV.inc(vector.tensor_tensor(
                        out=ln1b_s[:, dg, t0:t0 + TC],
                        in0=xT[:, dg, t0:t0 + TC], in1=pb[5][:, 0:TC],
                        op=ALU.mult))
            v_apply = sV.n
            bank_free[4] = (sV, v_apply)
            bank_free[5] = (sV, v_apply)

            @block.sync
            def _(sync, tc=tc, v_rstd=v_rstd):
                sA.wait(sync, v_rstd)
                sRS.inc(sync.dma_start(
                    out=rstd_d.ap()[tc * TC:(tc + 1) * TC].unsqueeze(0),
                    in_=rrow_s[0:1, :]), 16)
            v_rstdt[tc] = sRS.n
        v_ln1_v = sV.n

        @block.sync
        def _(sync):
            sRS.wait(sync, v_rstdt[NTC - 1])
            sRS.inc(sync.dma_start(
                out=rstdt_s[:],
                in_=rstd_d.ap().rearrange("(t p) -> p t", p=P),
                allow_slow_non_contiguous=True), 16)
        v_rstdt_load = sRS.n

        for tt in range(NTT):
            tc = (tt * P) // TC
            for dg in range(DG):
                @block.tensor
                def _(tensor, tt=tt, dg=dg):
                    bk = 2 + (dg % 2)
                    bwait_free(tensor, bk)
                    if tt == 0 and dg == 0:
                        sV.wait(tensor, v_ln1_v)
                    mm = tensor.transpose(pb[bk][:, 0:P],
                                          xT[:, dg, tt * P:(tt + 1) * P],
                                          id_s[:])
                    bank_prod[bk] = (sT, sT.inc(mm))

                @bvec
                def _(vector, tt=tt, dg=dg):
                    bk = 2 + (dg % 2)
                    bwait_prod(vector, bk)
                    if tt == 0 and dg == 0:
                        sRS.wait(vector, v_rstdt_load)
                    vector.tensor_single_scalar(
                        out=acc_s[:, tt, dg * P:(dg + 1) * P],
                        in_=pb[bk][:, 0:P], scalar=rstdt_s[:, tt:tt + 1],
                        op=ALU.mult)
                    bank_free[bk] = (sV, sV.n)

            @block.tensor
            def _(tensor, tt=tt):
                bwait_free(tensor, 6)
                for dg in range(DG):
                    tensor.matmul(pb[6][:, 0:NE], ln1b_s[:, dg, tt * P:(tt + 1) * P],
                                  wrb_s[:, dg, :], start=(dg == 0), stop=False)
                mm = tensor.matmul(pb[6][:, 0:NE], ones_all[0:1, :], br_s[:],
                                   start=False, stop=True)
                bank_prod[6] = (sT, sT.inc(mm))

            @bvec
            def _(vector):
                bwait_prod(vector, 6)
                sV.inc(vector.reduce_max(out=red_s[:, 0:1], in_=pb[6][:, 0:NE],
                                         axis=AX.X, negate=True))
            v_nm = sV.n

            @bscal
            def _(scalar, v_nm=v_nm):
                sV.wait(scalar, v_nm)
                e = scalar.activation(out=rscr_s[:, 0:NE], in_=pb[6][:, 0:NE],
                                      func=AF.Exp, bias=red_s[:, 0:1])
                bank_free[6] = (sA, sA.inc(e))
            v_e = sA.n

            @bvec
            def _(vector, tt=tt, v_e=v_e):
                sA.wait(vector, v_e)
                e = rscr_s[:, 0:NE]
                f = rscr_s[:, 8:8 + NE]
                w2 = rscr_s[:, 16:16 + NE]
                vector.reduce_max(out=red_s[:, 1:2], in_=e, axis=AX.X)
                vector.tensor_single_scalar(out=f, in_=e,
                                            scalar=red_s[:, 1:2],
                                            op=ALU.is_equal)
                vector.tensor_scalar(out=f, in0=f, scalar1=-1.0, scalar2=1.0,
                                     op0=ALU.mult, op1=ALU.add)
                vector.tensor_mul(w2, e, f)
                vector.reduce_max(out=red_s[:, 2:3], in_=w2, axis=AX.X)
                vector.tensor_single_scalar(out=f, in_=e,
                                            scalar=red_s[:, 2:3],
                                            op=ALU.is_ge)
                vector.tensor_mul(e, e, f)
                vector.reduce_sum(out=red_s[:, 3:4], in_=e, axis=AX.X)
                vector.reciprocal(out=red_s[:, 4:5], in_=red_s[:, 3:4])
                sV.inc(vector.tensor_scalar_mul(rtok_s[:, tt, :], e,
                                                red_s[:, 4:5]))
            v_rt = sV.n

            @block.tensor
            def _(tensor, tt=tt, v_rt=v_rt):
                bwait_free(tensor, 7)
                sV.wait(tensor, v_rt)
                mm = tensor.transpose(pb[7][0:NE, 0:P], rtok_s[:, tt, :],
                                      id_s[:])
                bank_prod[7] = (sT, sT.inc(mm))

            @bscal
            def _(scalar, tt=tt):
                bwait_prod(scalar, 7)
                cp = scalar.copy(out=rfm_s[:, tt * P:(tt + 1) * P],
                                 in_=pb[7][0:NE, 0:P])
                bank_free[7] = (sA, sA.inc(cp))
        v_accinit = sA.n
        v_rt_all = sV.n

        keep or lscope.close()
        keep or xT_scope.close()

        fscope = contextlib.ExitStack()
        Ef = fscope.enter_context
        stg = Ef(nc.sbuf_tensor("stg", [P, HQ], F32))
        w1qb = Ef(nc.sbuf_tensor("w1qb", [P, 2, DG, HQ], BF16))
        w2qb = Ef(nc.sbuf_tensor("w2qb", [P, 2, HC, DM], BF16))
        h_sb = Ef(nc.sbuf_tensor("h_sb", [P, HC, N5], BF16))
        out_s = Ef(nc.sbuf_tensor("out_s", [P, 2, DM], F32))

        NQT = NE * NQ
        W1B = (0, 1, 4, 5)
        EOB = (2, 3, 6, 7)
        v_w1_mmdone = {}
        v_w2_mmdone = {}
        v_castdone = {}
        v_b1 = {}
        v_b1_use = {}
        heread = {0: 0, 1: 0}
        cast_prev = [0]

        def emit_stream(QI):
            e, q = divmod(QI, NQ)
            slot = QI % 2
            if q == 0:
                @block.sync
                def _(sync, e=e, v=v_b1_use.get(e - 2, 0)):
                    if v:
                        sA.wait(sync, v)
                    sB1.inc(sync.dma_start(out=b1r_s[:, e % 2, :],
                                           in_=b1_h.ap()[e]), 16)
                v_b1[e] = sB1.n

            vfree = (v_w1_mmdone.get(QI - 2, 0), v_w2_mmdone.get(QI - 2, 0))
            for half in range(2):
                for ck in range(DG):
                    @block.sync
                    def _(sync, e=e, q=q, ck=ck, half=half, QI=QI,
                          vc=cast_prev[0]):
                        if QI == 0 and half == 0 and ck == 0:
                            sA.wait(sync, v_accinit)
                        if vc:
                            sG.wait(sync, vc)
                        if half == 0:
                            src_ap = w1_h.ap()[
                                e, ck * P:(ck + 1) * P,
                                q * HQ:(q + 1) * HQ]
                        else:
                            src_ap = w2_h.ap()[
                                e, q * HQ + ck * P:q * HQ + (ck + 1) * P, :]
                        sW1.inc(sync.dma_start(out=stg[:], in_=src_ap), 16)
                    v_dma = sW1.n

                    @bgps
                    def _(gpsimd, slot=slot, ck=ck, half=half, v_dma=v_dma,
                          vfree=vfree[half], first=(ck == 0)):
                        sW1.wait(gpsimd, v_dma)
                        if first and vfree:
                            sT.wait(gpsimd, vfree)
                        dst = (w1qb if half == 0 else w2qb)
                        gpsimd.tensor_copy(
                            out=dst[:, slot, ck, :], in_=stg[:])
                        cast_prev[0] = sG.n
            v_castdone[QI] = sG.n

        def emit_compute(QI):
            e, q = divmod(QI, NQ)
            slot = QI % 2
            b2eff = (e == 0 and q == 0)
            for t5 in range(NT5):
                t0 = t5 * N5
                gW = [0] * HC

                @block.tensor
                def _(tensor, e=e, q=q, QI=QI, t5=t5, slot=slot, t0=t0):
                    if t5 == 0:
                        sG.wait(tensor, v_castdone[QI])
                        if QI == 0:
                            sV.wait(tensor, v_ln1_v)
                    vrelu_base = sA.n
                    for hdc in range(HC):
                        bk = W1B[hdc % 4]
                        for dg in range(DG):
                            mm = tensor.matmul(
                                pb[bk][:, 0:N5],
                                w1qb[:, slot, dg, hdc * P:(hdc + 1) * P],
                                ln1b_s[:, dg, t0:t0 + N5],
                                start=(dg == 0), stop=(dg == DG - 1))
                            if dg == 0:
                                if hdc < 4:
                                    if bank_free[bk] is not None:
                                        c, v = bank_free[bk]
                                        mm._wait_ge(c.h, v)
                                else:
                                    mm._wait_ge(sA.h,
                                                vrelu_base + (hdc - 4) + 1)
                        gW[hdc] = sT.inc(mm)

                @bscal
                def _(scalar, e=e, q=q, vh=heread[0]):
                    for hdc in range(HC):
                        bk = W1B[hdc % 4]
                        if hdc == 0:
                            sB1.wait(scalar, v_b1[e])
                            if vh:
                                sT.wait(scalar, vh)
                        sT.wait(scalar, gW[hdc])
                        scalar.activation(
                            out=h_sb[:, hdc, :], in_=pb[bk][:, 0:N5],
                            func=AF.Relu,
                            bias=b1r_s[:, e % 2,
                                       q * HC + hdc:q * HC + hdc + 1])
                        bank_free[bk] = (sA, sA.n)
                v_hevac = sA.n
                v_w1_mmdone[QI] = sT.n
                mEO = [0] * (2 * T128)

                @block.tensor
                def _(tensor, e=e, QI=QI, t5=t5, slot=slot,
                      b2eff=b2eff, v_hevac=v_hevac):
                    sA.wait(tensor, v_hevac)
                    vstt_base = sV.n
                    for i in range(2 * T128):
                        t1, dh = divmod(i, 2)
                        ttg = t5 * T128 + t1
                        bk = EOB[i % 4]
                        for hdc in range(HC):
                            mm = tensor.matmul(
                                pb[bk][:, 0:512],
                                h_sb[:, hdc, t1 * P:(t1 + 1) * P],
                                w2qb[:, slot, hdc, dh * 512:(dh + 1) * 512],
                                start=(hdc == 0),
                                stop=(hdc == HC - 1 and not b2eff))
                            if hdc == 0:
                                if i < 4:
                                    if bank_free[bk] is not None:
                                        c, v = bank_free[bk]
                                        mm._wait_ge(c.h, v)
                                else:
                                    mm._wait_ge(sV.h,
                                                vstt_base + (i - 4) + 1)
                        if b2eff:
                            mm = tensor.matmul(
                                pb[bk][:, 0:512],
                                rfm_s[:, ttg * P:(ttg + 1) * P],
                                b2_s[:, dh * 512:(dh + 1) * 512],
                                start=False, stop=True)
                        mEO[i] = sT.inc(mm)
                heread[0] = sT.n
                v_w2_mmdone[QI] = sT.n

                @bvec
                def _(vector, e=e, QI=QI, t5=t5):
                    for i in range(2 * T128):
                        t1, dh = divmod(i, 2)
                        ttg = t5 * T128 + t1
                        bk = EOB[i % 4]
                        if i == 0 and QI == 0 and t5 == 0:
                            sA.wait(vector, v_accinit)
                            vector.wait_ge(sV.h, v_rt_all)
                        sT.wait(vector, mEO[i])
                        vector.scalar_tensor_tensor(
                            out=acc_s[:, ttg, dh * 512:(dh + 1) * 512],
                            in0=pb[bk][:, 0:512],
                            scalar=rtok_s[:, ttg, e:e + 1],
                            in1=acc_s[:, ttg, dh * 512:(dh + 1) * 512],
                            op0=ALU.mult, op1=ALU.add)
                        bank_free[bk] = (sV, sV.n)
            v_b1_use[e] = v_hevac

        emit_stream(0)
        for QI in range(NQT):
            if QI + 1 < NQT:
                emit_stream(QI + 1)
            emit_compute(QI)

        v_ffn_v = sV.n

        v_outdma = [0] * NTT
        for tt in range(NTT):
            @bvec
            def _(vector, tt=tt):
                if tt == 0:
                    sV.wait(vector, v_ffn_v)
                vector.reduce_sum(out=red_s[:, 0:1], in_=acc_s[:, tt, :],
                                  axis=AX.X)
                vector.tensor_scalar_mul(red_s[:, 1:2], red_s[:, 0:1],
                                         1.0 / DM)
                if tt >= 2:
                    sO[tt % 2].wait(vector, v_outdma[tt - 2])
                vector.scalar_tensor_tensor(
                    out=out_s[:, tt % 2, :], in0=acc_s[:, tt, :], scalar=1.0,
                    in1=acc_s[:, tt, :], op0=ALU.mult, op1=ALU.mult,
                    accum_out=red_s[:, 2:3])
                vector.tensor_scalar_mul(red_s[:, 3:4], red_s[:, 2:3],
                                         1.0 / DM)
                vector.tensor_mul(red_s[:, 4:5], red_s[:, 1:2], red_s[:, 1:2])
                sV.inc(vector.tensor_sub(red_s[:, 5:6], red_s[:, 3:4],
                                         red_s[:, 4:5]))
            v_var2 = sV.n

            @bscal
            def _(scalar, v_var2=v_var2):
                sV.wait(scalar, v_var2)
                scalar.activation(out=red_s[:, 6:7], in_=red_s[:, 5:6],
                                  func=AF.Ln, bias=eps_s[:, 0:1])
                sA.inc(scalar.activation(out=red_s[:, 6:7], in_=red_s[:, 6:7],
                                         func=AF.Exp, scale=-0.5))
            v_rstd2 = sA.n

            @bvec
            def _(vector, v_rstd2=v_rstd2):
                sA.wait(vector, v_rstd2)
                sV.inc(vector.scalar_tensor_tensor(
                    out=red_s[:, 7:8], in0=red_s[:, 1:2], scalar=-1.0,
                    in1=red_s[:, 6:7], op0=ALU.mult, op1=ALU.mult))
            v_nmr = sV.n

            @bvec
            def _(vector, tt=tt, v_nmr=v_nmr):
                vector.wait_ge(sV.h, v_nmr)
                nmb = red_s[:, 7:8].broadcast_to([P, DM])
                sV.inc(vector.scalar_tensor_tensor(
                    out=out_s[:, tt % 2, :], in0=acc_s[:, tt, :],
                    scalar=red_s[:, 6:7], in1=nmb,
                    op0=ALU.mult, op1=ALU.add))
            v_oev = sV.n

            @bgps
            def _(gpsimd, tt=tt, v_oev=v_oev):
                sV.wait(gpsimd, v_oev)
                sO[tt % 2].inc(gpsimd.dma_start(
                    out=out_h.ap()[tt * P:(tt + 1) * P, :],
                    in_=out_s[:, tt % 2, :]), 16)
            v_outdma[tt] = sO[tt % 2].n

        @bgps
        def _(gpsimd):
            sO[0].wait(gpsimd)
            sO[1].wait(gpsimd)

        keep or fscope.close()
        if keep:
            for sc in (fscope, lscope, sscope, xscope, mscope, xT_scope):
                sc.close()

    return nc


def prepare_in_maps(L0_attn_x, L0_attn_A_log, L0_attn_D, L0_attn_Wx,
                    L0_attn_dt_W, L0_attn_dt_b, L0_ffn_W1, L0_ffn_b1,
                    L0_ffn_W2, L0_ffn_b2, L0_ffn_Wr, L0_ffn_br):
    CL = L0_attn_x.shape[1] // NCORE
    x = np.ascontiguousarray(L0_attn_x, dtype=np.float32)
    c = np.ascontiguousarray

    def pd(v):
        v = np.asarray(v, dtype=np.float32)
        return c(v.reshape(DG, P, *v.shape[1:]).transpose(
            1, 0, *range(2, v.ndim + 1)))

    shared = {
        "WxT": c(np.asarray(L0_attn_Wx, np.float32).T),
        "dtWT": c(np.asarray(L0_attn_dt_W, np.float32).T),
        "A_pd": pd(L0_attn_A_log),
        "D_pd": pd(L0_attn_D),
        "dtb_pd": pd(L0_attn_dt_b),
        "ident": np.eye(P, dtype=np.float32),
        "Wr": c(np.asarray(L0_ffn_Wr, np.float32)),
        "br_r": c(np.asarray(L0_ffn_br, np.float32).reshape(1, NE)),
        "b2": c(np.asarray(L0_ffn_b2, np.float32)),
        "b1r": c(np.asarray(L0_ffn_b1, np.float32).reshape(
            NE, H // P, P).transpose(0, 2, 1)),
        "W1": c(np.asarray(L0_ffn_W1, np.float32)),
        "W2": c(np.asarray(L0_ffn_W2, np.float32)),
    }

    in_maps = []
    for cid in range(NCORE):
        m = dict(shared)
        m["x01"] = c(np.concatenate(
            [x[0, cid * CL:(cid + 1) * CL], x[1, cid * CL:(cid + 1) * CL]],
            axis=0))
        m["maskv"] = c(np.broadcast_to(
            (np.arange(NCORE) < cid).astype(np.float32), (P, NCORE)))
        in_maps.append(m)
    return in_maps, CL


def assemble_out(results, CL):
    out = np.empty((NB, NCORE * CL, DM), dtype=np.float32)
    for cid in range(NCORE):
        o = results[cid]["out"]
        out[0, cid * CL:(cid + 1) * CL] = o[:CL]
        out[1, cid * CL:(cid + 1) * CL] = o[CL:]
    return out


def kernel(L0_attn_x, L0_attn_A_log, L0_attn_D, L0_attn_Wx, L0_attn_dt_W,
           L0_attn_dt_b, L0_ffn_W1, L0_ffn_b1, L0_ffn_W2, L0_ffn_b2,
           L0_ffn_Wr, L0_ffn_br):
    in_maps, CL = prepare_in_maps(
        L0_attn_x, L0_attn_A_log, L0_attn_D, L0_attn_Wx, L0_attn_dt_W,
        L0_attn_dt_b, L0_ffn_W1, L0_ffn_b1, L0_ffn_W2, L0_ffn_b2,
        L0_ffn_Wr, L0_ffn_br)
    nc = build_nc(CL=CL)
    res = run_bass_kernel_spmd(nc, in_maps, core_ids=list(range(NCORE)))
    return assemble_out(res.results, CL)
